# version 10
# baseline (speedup 1.0000x reference)
import numpy as np
from contextlib import ExitStack

try:
    import concourse.bass as bass
except ImportError:
    import sys
    sys.path.insert(0, "/opt/trn_rl_repo")
    import concourse.bass as bass
import concourse.mybir as mybir
from concourse.bass_utils import run_bass_kernel_spmd
from concourse.bass_isa import ReduceOp
from concourse import library_config

F32 = mybir.dt.float32
BF16 = mybir.dt.bfloat16
FP8 = mybir.dt.float8e4
AF = mybir.ActivationFunctionType
ALU = mybir.AluOpType
AX = mybir.AxisListType

B, T, C, H, D, DH = 4, 4096, 1024, 16, 64, 4096
NT = T // 128
OWN = T // 2
MAGIC = float(np.float32(1.5 * 2 ** 23))
EPS = 1e-5
QCH = 512
NQC = T // QCH
FCH = 128
NFC = OWN // FCH
NCH = NFC + 1
CLIP = 1e-5
SUM_FLOOR = 1e-30

STAGE = 99

SEMNAMES = ("dx", "dc", "dw", "df", "dcc", "dy", "dres", "day",
            "a1", "a2", "a3", "a4", "a5", "a6", "a7", "a8",
            "b1", "b2", "b3", "b4", "b5", "b6", "b7", "b8",
            "c1", "c2", "c3", "c4", "c5", "ph", "dg1", "dbg", "p1", "p2", "p3")


def build_program(phases=4, dbg=()):
    nc = bass.Bass("TRN2", target_bir_lowering=False, debug=False,
                   detect_race_conditions=False)
    st = ExitStack()

    d_x = nc.dram_tensor("x", [T, C], F32, kind="ExternalInput")
    d_xffn = nc.dram_tensor("xffn", [2432, C], F32, kind="ExternalInput")
    d_wqkvT = nc.dram_tensor("wqkvT", [C, 1536], F32, kind="ExternalInput")
    d_bqkv = nc.dram_tensor("bqkv", [1, 1536], F32, kind="ExternalInput")
    d_woT = nc.dram_tensor("woT", [2048, C], F32, kind="ExternalInput")
    d_bo = nc.dram_tensor("bo_bc", [128, C], F32, kind="ExternalInput")
    d_w1T = nc.dram_tensor("w1T", [C, DH], F32, kind="ExternalInput")
    d_w2T = nc.dram_tensor("w2T", [DH, C], F32, kind="ExternalInput")
    d_cwb = nc.dram_tensor("convwb", [128, 128], F32, kind="ExternalInput")
    d_ropeC = nc.dram_tensor("ropeC", [128, NT * D], F32, kind="ExternalInput")
    d_ropeS = nc.dram_tensor("ropeS", [128, NT * D], F32, kind="ExternalInput")
    d_mask = nc.dram_tensor("masktri", [128, 128], BF16, kind="ExternalInput")
    d_idb = nc.dram_tensor("ident_bf", [128, 128], BF16, kind="ExternalInput")
    d_idf = nc.dram_tensor("ident_f32", [128, 128], F32, kind="ExternalInput")
    d_y = nc.dram_tensor("y", [OWN, C], F32, kind="ExternalOutput")
    d_ccin = nc.dram_tensor("cc_in", [2, 4, 128, OWN], BF16)
    d_ccoutA = nc.dram_tensor("cc_outA", [2, 4, 128, OWN], BF16)
    d_ccoutB = nc.dram_tensor("cc_outB", [2, 4, 128, OWN], BF16)
    d_atny = nc.dram_tensor("attnY", [OWN, C], F32)

    dbg_t = {}
    if "ln" in dbg:
        dbg_t["xnb0"] = nc.dram_tensor("dbg_xnb0", [128, C], BF16, kind="ExternalOutput")
        dbg_t["xnb1"] = nc.dram_tensor("dbg_xnb1", [128, C], BF16, kind="ExternalOutput")
        dbg_t["xnt0"] = nc.dram_tensor("dbg_xnt0", [128, C], BF16, kind="ExternalOutput")
        dbg_t["qkr0"] = nc.dram_tensor("dbg_qkr0", [128, C], BF16, kind="ExternalOutput")
        dbg_t["mv"] = nc.dram_tensor("dbg_mv", [128, 2], F32, kind="ExternalOutput")
        dbg_t["rs"] = nc.dram_tensor("dbg_rs", [128, 1], F32, kind="ExternalOutput")
        dbg_t["sums"] = nc.dram_tensor("dbg_sums", [128, 1], F32, kind="ExternalOutput")
        dbg_t["sumsq"] = nc.dram_tensor("dbg_sumsq", [128, 1], F32, kind="ExternalOutput")
        dbg_t["veps"] = nc.dram_tensor("dbg_veps", [128, 1], F32, kind="ExternalOutput")
        dbg_t["sqx"] = nc.dram_tensor("dbg_sqx", [128, C], F32, kind="ExternalOutput")
    if "qkv" in dbg:
        dbg_t["qt"] = nc.dram_tensor("dbg_qt", [4, 128, T], BF16, kind="ExternalOutput")
        dbg_t["kt"] = nc.dram_tensor("dbg_kt", [4, 128, T], BF16, kind="ExternalOutput")
        dbg_t["vaug"] = nc.dram_tensor("dbg_vaug", [128, NT, 8, 65], BF16, kind="ExternalOutput")
        dbg_t["negc"] = nc.dram_tensor("dbg_negc", [128, 8], F32, kind="ExternalOutput")
    if "attn" in dbg:
        dbg_t["ccin"] = nc.dram_tensor("dbg_ccin", [2, 4, 128, OWN], BF16, kind="ExternalOutput")
    if "oproj" in dbg:
        dbg_t["atny"] = nc.dram_tensor("dbg_atny", [OWN, C], F32, kind="ExternalOutput")
    if "ffn" in dbg:
        dbg_t["w1q"] = nc.dram_tensor("dbg_w1q", [128, 8, DH], F32, kind="ExternalOutput")
        dbg_t["w2q"] = nc.dram_tensor("dbg_w2q", [128, 32, C], F32, kind="ExternalOutput")
        dbg_t["h1"] = nc.dram_tensor("dbg_h1", [128, 32, FCH], F32, kind="ExternalOutput")
        dbg_t["hq0"] = nc.dram_tensor("dbg_hq0", [128, 32, FCH], BF16, kind="ExternalOutput")

    g = {}
    def sb(name, shape, dt):
        g[name] = st.enter_context(nc.sbuf_tensor(name, shape, dt))
        return g[name]

    sb("idb", [128, 128], BF16)
    sb("idf", [128, 128], F32)
    sb("mask", [128, 128], BF16)
    sb("cwb", [128, 128], F32)
    sb("bo_bc_s", [128, C], F32)
    sb("bqkv_s", [1, 1536], BF16)
    sb("ones_bf", [128, 128], BF16)
    sb("ones_f", [128, 128], F32)
    sb("nq2", [128, 8], F32)
    sb("nk2", [128, 8], F32)
    sb("negc", [128, 8], F32)
    sb("mw1", [128, 1], F32)
    sb("mw2", [128, 1], F32)
    sb("sw1", [128, 1], F32)
    sb("sw2", [128, 1], F32)
    sb("smallA", [128, 8], F32)
    sb("magic", [128, 1], F32)
    sb("barv", [1, 1], F32)
    sb("bara", [1, 1], F32)

    sm = {n: st.enter_context(nc.semaphore(n)) for n in SEMNAMES}
    cnt = {n: 0 for n in SEMNAMES}

    def inc(instr, name, k=1):
        instr.then_inc(sm[name], k)
        cnt[name] += k
        return instr

    class Rec:
        def __init__(self, e):
            self._e = e
            self.last = None
        def __getattr__(self, n):
            f = getattr(self._e, n)
            if not callable(f):
                return f
            def w(*a, **k):
                r = f(*a, **k)
                if n not in ("wait_ge", "nop", "dma_start") and hasattr(r, "then_inc"):
                    self.last = r
                return r
            return w

    def incl(rec, name, k=1, drain=False):
        e = rec._e
        if hasattr(e, "activation"):
            rec.last.then_inc(sm[name], k)
        elif hasattr(e, "tensor_tensor") and not hasattr(e, "matmul"):
            e.memset(g["barv"][:], 0.0).then_inc(sm[name], k)
        else:
            rec.last.then_inc(sm[name], k)
        cnt[name] += k

    stB = ExitStack()
    for nm, shape, dt in (("QT", [128, 4, T], BF16), ("KT", [128, 4, T], BF16),
                          ("VA", [128, NT, 8, 65], BF16)):
        g[nm] = stB.enter_context(nc.sbuf_tensor(nm, shape, dt))

    ST = STAGE
    blk = st.enter_context(nc.Block())
    PHW = [0]

    def barrier_close(engines=("sync", "tensor", "vector", "scalar", "gpsimd")):
        for e in engines:
            def mk(e):
                def f(eng):
                    if e == "vector":
                        inc(eng.memset(g["barv"][:], 0.0), "ph")
                    elif e == "scalar":
                        inc(eng.memzero(g["bara"][:]), "ph")
                    else:
                        inc(eng.nop(), "ph")
                return f
            getattr(blk, e)(mk(e))
        PHW[0] += len(engines)

    def bwait(eng):
        if PHW[0]:
            eng.wait_ge(sm["ph"], PHW[0])

    def ph01():
        p1 = ExitStack()
        def p1sb(name, shape, dt):
            return p1.enter_context(nc.sbuf_tensor(name, shape, dt))
        ropeC = p1sb("ropeCt", [128, NT, D], F32)
        ropeS = p1sb("ropeSt", [128, NT, D], F32)
        wqkv = p1sb("wqkv", [128, 8, 1536], BF16)
        xb = [p1sb(f"xb{i}", [128, C], F32) for i in range(2)]
        xnb = [p1sb(f"xnb{i}", [128, C], BF16) for i in range(2)]
        xnT = [p1sb(f"xnT{i}", [128, C], BF16) for i in range(2)]
        qkr = [p1sb(f"qkr{i}", [128, C], BF16) for i in range(2)]
        t2 = p1sb("t2", [128, C], F32)
        t3 = p1sb("t3", [128, C], F32)
        sqx = p1sb("sqx", [128, C], F32)
        sums = p1sb("sums", [128, 1], F32)
        sumsq = p1sb("sumsq", [128, 1], F32)
        mv = p1sb("mv", [128, 2], F32)
        veps = p1sb("veps", [128, 1], F32)
        lnv = p1sb("lnv", [128, 1], F32)
        rs = p1sb("rs", [128, 1], F32)
        nmrs = p1sb("nmrs", [128, 1], F32)
        dg = p1sb("dg", [128, 8], F32)
        sqb = [p1sb(f"sqb{i}", [128, C], F32) for i in range(2)]
        scr128 = p1sb("scr128", [128, 128], F32)
        prod8 = p1sb("prod8", [8, 1], F32)
        mq2c = p1sb("mq2c", [8, 1], F32)
        negc_col = p1sb("negc_col", [8, 1], F32)
        negc_row = p1sb("negc_row", [1, 8], F32)
        sumsB1 = p1sb("sumsB1", [128, NT], F32)
        sumsqB1 = p1sb("sumsqB1", [128, NT], F32)
        mvB1 = p1sb("mvB1", [128, NT], F32)
        vepsB1 = p1sb("vepsB1", [128, NT], F32)
        lnvB1 = p1sb("lnvB1", [128, NT], F32)
        rsB1 = p1sb("rsB1", [128, NT], F32)
        nmrsB1 = p1sb("nmrsB1", [128, NT], F32)

        pp = ExitStack()
        ps_qk = [pp.enter_context(nc.psum_tensor(f"ps_qk{i}", [128, 1024], F32)) for i in range(2)]
        ps_v = pp.enter_context(nc.psum_tensor("ps_v", [128, 512], F32))
        ps_trx = pp.enter_context(nc.psum_tensor("ps_trx", [128, 1024], F32))
        ps_trn = pp.enter_context(nc.psum_tensor("ps_trn", [128, 128], F32))
        ps_trA = ps_trx[:, 0:512].rearrange("p q -> p q") if False else None
        ps_trB = None

        @blk.sync
        def _(sy):
            sy = Rec(sy)
            sy.dma_start(out=g["idb"][:], in_=d_idb[:]).then_inc(sm["dc"], 16)
            sy.dma_start(out=g["idf"][:], in_=d_idf[:]).then_inc(sm["dc"], 16)
            sy.dma_start(out=g["mask"][:], in_=d_mask[:]).then_inc(sm["dc"], 16)
            sy.dma_start(out=g["cwb"][:], in_=d_cwb[:]).then_inc(sm["dc"], 16)
            sy.dma_start(out=g["bo_bc_s"][:], in_=d_bo[:]).then_inc(sm["dc"], 16)
            sy.dma_start(out=ropeC[:], in_=d_ropeC.ap().rearrange("p (t d) -> p t d", d=D)).then_inc(sm["dc"], 16)
            sy.dma_start(out=ropeS[:], in_=d_ropeS.ap().rearrange("p (t d) -> p t d", d=D)).then_inc(sm["dc"], 16)
            cnt["dc"] += 7 * 16
            for t in range(0, NT, 2):
                if t >= 2:
                    sy.wait_ge(sm["p3"], t - 1)
                sy.dma_start(out=xb[t % 2][:], in_=d_x[t * 128:(t + 1) * 128, :]).then_inc(sm["p2"], 16)
                cnt["p2"] += 16
            for t in range(NT):
                if t < 2:
                    sy.wait_ge(sm["p3"], NT)
                else:
                    sy.wait_ge(sm["a4"], t - 1)
                sy.dma_start(out=xb[t % 2][:], in_=d_x[t * 128:(t + 1) * 128, :]).then_inc(sm["dx"], 16)
                cnt["dx"] += 16
            sy.wait_ge(sm["dx"], cnt["dx"])

        @blk.gpsimd
        def _(gp):
            gp = Rec(gp)
            gp.dma_start(out=g["bqkv_s"][:], in_=d_bqkv[:]).then_inc(sm["dg1"], 16)
            cnt["dg1"] += 16
            for i in range(8):
                gp.dma_start(out=wqkv[:, i, :], in_=d_wqkvT[i * 128:(i + 1) * 128, :]).then_inc(sm["dg1"], 16)
            cnt["dg1"] += 8 * 16
            for t in range(1, NT, 2):
                gp.wait_ge(sm["p3"], t - 1)
                gp.dma_start(out=xb[t % 2][:], in_=d_x[t * 128:(t + 1) * 128, :]).then_inc(sm["dg1"], 16)
                cnt["dg1"] += 16
            gp.wait_ge(sm["dg1"], cnt["dg1"])


        dc_all = cnt["dc"]
        dg1_all = cnt["dg1"]

        @blk.vector
        def _(v):
            v = Rec(v)
            v.memset(g["ones_bf"][:], 1.0)
            v.memset(g["ones_f"][:], 1.0)
            v.memset(g["magic"][:], MAGIC)
            v.memset(g["nq2"][:], 0.0)
            v.memset(g["nk2"][:], 0.0)
            v.memset(g["VA"][:, :, :, 64:65], 1.0)
            incl(v, "b1")
            sqpp = [sqx, t2]
            for t in range(NT):
                if t % 2 == 0:
                    v.wait_ge(sm["p2"], 16 * (t // 2 + 1))
                else:
                    v.wait_ge(sm["dg1"], 144 + 16 * ((t + 1) // 2))
                v.reduce_sum(sumsB1[:, t:t + 1], xb[t % 2][:], AX.X)
                v.wait_ge(sm["p1"], t + 1)
                v.reduce_sum(sumsqB1[:, t:t + 1], sqpp[t % 2][:], AX.X)
                incl(v, "p3", drain=True)
            v.tensor_scalar(mvB1[:], sumsB1[:], 1.0 / C, None, ALU.mult)
            v.drain()
            v.tensor_tensor(vepsB1[:], mvB1[:], mvB1[:], ALU.mult)
            v.drain()
            v.scalar_tensor_tensor(vepsB1[:], sumsqB1[:], 1.0 / C, vepsB1[:],
                                   ALU.mult, ALU.subtract)
            v.drain()
            v.tensor_scalar(vepsB1[:], vepsB1[:], EPS, None, ALU.add)
            incl(v, "a1", drain=True)
            v.wait_ge(sm["p1"], NT + 1)
            v.scalar_tensor_tensor(nmrsB1[:], mvB1[:], -1.0, rsB1[:], ALU.mult, ALU.mult)
            incl(v, "a3", drain=True)
            for t in range(NT):
                v.wait_ge(sm["a5"], t + 1)
                if ST >= 7:
                    v.tensor_copy(xnT[t % 2][:, 0:512], ps_trx[:, 0:512])
                    v.tensor_copy(xnT[t % 2][:, 512:1024], ps_trx[:, 512:1024])
                incl(v, "a6", drain=True)
                v.wait_ge(sm["a7"], t + 1)
                if t >= 2:
                    v.wait_ge(sm["b2"], t - 1)
                    v.wait_ge(sm["c3"], t - 1)
                crep = ropeC[:, t:t + 1, :].broadcast_to([128, 16, D])
                srep = ropeS[:, t:t + 1, :].broadcast_to([128, 16, D])
                qk = ps_qk[t % 2][:].rearrange("p (r d) -> p r d", d=D)
                v.tensor_tensor(t2[:].rearrange("p (r d) -> p r d", d=D), qk, crep, ALU.mult)
                qk_sw = bass.AP(ps_qk[t % 2][:].tensor, 1, [[1024, 128], [64, 16], [2, 32], [-1, 2]])
                srep4 = ropeS[:, t, :].rearrange("p (a b) -> p a b", b=2).unsqueeze(1).broadcast_to([128, 16, 32, 2])
                v.tensor_tensor(t3[:].rearrange("p (r a b) -> p r a b", a=32, b=2), qk_sw,
                                srep4, ALU.mult)
                v.tensor_add(qkr[t % 2][:], t2[:], t3[:])
                v.tensor_copy(g["VA"][:, t, :, 0:64],
                              ps_v[:].rearrange("p (h d) -> p h d", d=64))
                incl(v, "a8")
                v.wait_ge(sm["b2"], t + 1)
                if ST >= 12:
                    for b_ in range(4):
                        v.tensor_copy(g["QT"][:, b_, t * 128:(t + 1) * 128],
                                      ps_trx[:, b_ * 128:(b_ + 1) * 128])
                    for b_ in range(4):
                        v.tensor_copy(g["KT"][:, b_, t * 128:(t + 1) * 128],
                                      ps_trx[:, 512 + b_ * 128:512 + (b_ + 1) * 128])
                incl(v, "b3", drain=True)
                v.wait_ge(sm["c3"], t + 1)
                if ST >= 13:
                    sq_ = sqb[t % 2][:]
                    v.tensor_reduce(dg[:].unsqueeze(2), sq_[:, 0:512].rearrange("p (h d) -> p h d", d=64),
                                    AX.X, ALU.add)
                    v.drain()
                    v.tensor_tensor(g["nq2"][:], g["nq2"][:], dg[:], ALU.max)
                    v.tensor_reduce(dg[:].unsqueeze(2), sq_[:, 512:1024].rearrange("p (h d) -> p h d", d=64),
                                    AX.X, ALU.add)
                    v.drain()
                    v.tensor_tensor(g["nk2"][:], g["nk2"][:], dg[:], ALU.max)
                incl(v, "c4", drain=True)
            if ST >= 14:
                v.memset(scr128[:], 0.0)
                v.drain()
                v.tensor_copy(scr128[:, 0:8], g["nq2"][:])
            incl(v, "b6", drain=True)
            v.wait_ge(sm["c1"], 1)
            if ST >= 14:
                v.tensor_reduce(mq2c[:], ps_trn[0:8, :], AX.X, ALU.max)
                v.memset(scr128[:], 0.0)
                v.drain()
                v.tensor_copy(scr128[:, 0:8], g["nk2"][:])
            incl(v, "b6", drain=True)
            v.wait_ge(sm["c1"], 2)
            if ST >= 14:
                v.tensor_reduce(prod8[:], ps_trn[0:8, :], AX.X, ALU.max)
                v.drain()
                v.tensor_tensor(prod8[:], prod8[:], mq2c[:], ALU.mult)
            incl(v, "c4", drain=True)
            v.wait_ge(sm["b7"], 1)
            if ST >= 14:
                v.tensor_scalar(negc_col[:], negc_col[:], -1.02 / 8.0, 45.0, ALU.mult, ALU.add)
                v.memset(scr128[:], 0.0)
                v.drain()
                v.tensor_copy(scr128[0:8, 0:1], negc_col[:])
            incl(v, "c2", drain=True)
            v.wait_ge(sm["c1"], 3)
            if ST >= 14:
                v.tensor_copy(negc_row[:], ps_trn[0:1, 0:8])
            incl(v, "c2", drain=True)
            v.wait_ge(sm["c1"], 4)
            if ST >= 14:
                v.tensor_copy(g["negc"][:], ps_trn[:, 0:8])
            incl(v, "b8", drain=True)

        @blk.scalar
        def _(a):
            a = Rec(a)
            sqpp = [sqx, t2]
            for t in range(NT):
                if t % 2 == 0:
                    a.wait_ge(sm["p2"], 16 * (t // 2 + 1))
                else:
                    a.wait_ge(sm["dg1"], 144 + 16 * ((t + 1) // 2))
                if t >= 2:
                    a.wait_ge(sm["p3"], t - 1)
                a.activation(sqpp[t % 2][:], xb[t % 2][:], AF.Square)
                incl(a, "p1", drain=True)
            a.wait_ge(sm["a1"], 1)
            a.activation(lnvB1[:], vepsB1[:], AF.Ln)
            a.drain()
            a.activation(rsB1[:], lnvB1[:], AF.Exp, scale=-0.5)
            incl(a, "p1", drain=True)
            for t in range(NT):
                a.wait_ge(sm["dx"], 16 * min(t + 2, NT))
                a.wait_ge(sm["a3"], 1)
                if t >= 2:
                    a.wait_ge(sm["a5"], t - 1)
                if ST >= 5:
                    a.activation(xnb[t % 2][:], xb[t % 2][:], AF.Identity,
                                 bias=nmrsB1[:, t:t + 1], scale=rsB1[:, t:t + 1])
                incl(a, "a4", drain=True)
                a.wait_ge(sm["a8"], t + 1)
                if t >= 2:
                    a.wait_ge(sm["c4"], t - 1)
                if ST >= 10:
                    a.activation(sqb[t % 2][:], qkr[t % 2][:], AF.Square)
                incl(a, "c3", drain=True)
            a.wait_ge(sm["c4"], NT + 1)
            if ST >= 14:
                a.activation(negc_col[:], prod8[:], AF.Sqrt)
            incl(a, "b7", drain=True)

        @blk.tensor
        def _(te):
            te = Rec(te)
            te.wait_ge(sm["dc"], dc_all)
            te.wait_ge(sm["dg1"], dg1_all)
            te.wait_ge(sm["b1"], 1)
            for t in range(NT):
                te.wait_ge(sm["a4"], t + 1)
                if t >= 1:
                    te.wait_ge(sm["b3"], t)
                if ST >= 6:
                    for b_ in range(8):
                        te.matmul(ps_trx[:, b_ * 128:(b_ + 1) * 128],
                                  xnb[t % 2][:, b_ * 128:(b_ + 1) * 128], g["idb"][:],
                                  start=True, stop=True)
                incl(te, "a5")
                te.wait_ge(sm["a6"], t + 1)
                if t >= 2:
                    te.wait_ge(sm["a8"], t - 1)
                if ST >= 8:
                    for dk in range(8):
                        for n in range(2):
                            te.matmul(ps_qk[t % 2][:, n * 512:(n + 1) * 512],
                                      xnT[t % 2][:, dk * 128:(dk + 1) * 128],
                                      wqkv[:, dk, n * 512:(n + 1) * 512],
                                      start=(dk == 0), stop=False)
                    for n in range(2):
                        te.matmul(ps_qk[t % 2][:, n * 512:(n + 1) * 512],
                                  g["ones_bf"][0:1, :],
                                  g["bqkv_s"][0:1, n * 512:(n + 1) * 512],
                                  start=False, stop=True)
                    if t >= 1:
                        te.wait_ge(sm["a8"], t)
                    for dk in range(8):
                        te.matmul(ps_v[:],
                                  xnT[t % 2][:, dk * 128:(dk + 1) * 128],
                                  wqkv[:, dk, 1024:1536],
                                  start=(dk == 0), stop=False)
                    te.matmul(ps_v[:],
                              g["ones_bf"][0:1, :],
                              g["bqkv_s"][0:1, 1024:1536],
                              start=False, stop=True)
                incl(te, "a7")
                te.wait_ge(sm["a8"], t + 1)
                te.wait_ge(sm["a6"], t + 1)
                if ST >= 11:
                    for b_ in range(8):
                        te.matmul(ps_trx[:, b_ * 128:(b_ + 1) * 128],
                                  qkr[t % 2][:, b_ * 128:(b_ + 1) * 128], g["idb"][:],
                                  start=True, stop=True)
                incl(te, "b2")
            te.wait_ge(sm["b6"], 1)
            if ST >= 14:
                te.matmul(ps_trn[:], scr128[:], g["idf"][:], start=True, stop=True)
            incl(te, "c1")
            te.wait_ge(sm["b6"], 2)
            if ST >= 14:
                te.matmul(ps_trn[:], scr128[:], g["idf"][:], start=True, stop=True)
            incl(te, "c1")
            te.wait_ge(sm["c2"], 1)
            if ST >= 14:
                te.matmul(ps_trn[:], scr128[:], g["idf"][:], start=True, stop=True)
            incl(te, "c1")
            te.wait_ge(sm["c2"], 2)
            if ST >= 14:
                te.matmul(ps_trn[:, 0:8], g["ones_f"][0:1, 0:128], negc_row[:],
                          start=True, stop=True)
            incl(te, "c1")

        if "ln" in dbg:
            @blk.gpsimd
            def _(gp):
                gp.wait_ge(sm["b8"], 1)
                for nm, src in (("xnb0", xnb[0][:]), ("xnb1", xnb[1][:]), ("xnt0", xnT[0][:]),
                                ("qkr0", qkr[0][:]), ("mv", mv[:]), ("rs", rs[:]),
                                ("sums", sums[:]), ("sumsq", sumsq[:]), ("veps", veps[:]),
                                ("sqx", sqx[:])):
                    gp.dma_start(out=dbg_t[nm].ap(), in_=src).then_inc(sm["dbg"], 16)
                    cnt["dbg"] += 16
                gp.wait_ge(sm["dbg"], cnt["dbg"])

        if "qkv" in dbg:
            @blk.gpsimd
            def _(gp):
                gp = Rec(gp)
                gp.wait_ge(sm["b8"], 1)
                gp.wait_ge(sm["a8"], NT)
                for b_ in range(4):
                    gp.dma_start(out=dbg_t["qt"][b_], in_=g["QT"][:, b_, :]).then_inc(sm["dbg"], 16)
                    gp.dma_start(out=dbg_t["kt"][b_], in_=g["KT"][:, b_, :]).then_inc(sm["dbg"], 16)
                gp.dma_start(out=dbg_t["vaug"][:], in_=g["VA"][:]).then_inc(sm["dbg"], 16)
                gp.dma_start(out=dbg_t["negc"][:], in_=g["negc"][:]).then_inc(sm["dbg"], 16)
                cnt["dbg"] += 160
                gp.wait_ge(sm["dbg"], cnt["dbg"])

        barrier_close()
        pp.close()
        p1.close()

    ph01()
    if phases < 2:
        stB.close()
        st.close()
        return nc
    g["w1qT"] = st.enter_context(nc.sbuf_tensor("w1qT", [128, 8, DH], FP8, side="right"))

    def ph2():
        bw = PHW[0]
        p2 = ExitStack()
        def p2sb(name, shape, dt):
            return p2.enter_context(nc.sbuf_tensor(name, shape, dt))
        PT = [[p2sb(f"PT{a}{i}", [128, QCH], BF16) for i in range(4)] for a in range(2)]
        uaT = [[p2sb(f"uaT{a}{i}", [65, QCH], F32) for i in range(2)] for a in range(2)]
        rcp = [[p2sb(f"rcp{a}{i}", [65, QCH], F32) for i in range(2)] for a in range(2)]
        stg = [p2sb(f"stg{a}", [64, QCH], BF16) for a in range(2)]
        wbuf = [p2sb(f"wbuf{i}", [128, 2048], F32) for i in range(2)]
        utmp = p2sb("utmp", [128, 2048], F32)
        wsum = p2sb("wsum", [128, 1], F32)
        wpart = p2sb("wpart", [128, 1], F32)
        tsum = p2sb("tsum", [1, 1], F32)

        pp = ExitStack()
        ps_S = [[pp.enter_context(nc.psum_tensor(f"ps_S{a}{i}", [128, QCH], F32))
                 for i in range(2)] for a in range(2)]
        ps_oA = pp.enter_context(nc.psum_tensor("ps_oA", [65, QCH], F32))
        ps_oB = pp.enter_context(nc.psum_tensor("ps_oB", [65, QCH], F32))
        ps_bc = pp.enter_context(nc.psum_tensor("ps_bc", [64, QCH], F32))
        ps_ws = pp.enter_context(nc.psum_tensor("ps_ws", [128, 2], F32))

        iters = []
        for qc in range(NQC):
            nkb = (qc + 1) * (QCH // 128)
            kbs0 = list(range(4 * qc, nkb)) + list(range(0, 4 * qc))
            for pj in range(4):
                iters.append((pj, qc, list(kbs0)))
        kb_list = []
        prevpar = []
        prevpt = []
        lastpar = {0: None, 1: None}
        lastpt = {i: None for i in range(4)}
        diag_of = {}
        for it, (pj, qc, kbs) in enumerate(iters):
            for kb in kbs:
                qlo = max(qc * QCH, kb * 128)
                w = qc * QCH + QCH - qlo
                diag = kb * 128 >= qc * QCH
                ki = len(kb_list)
                if diag:
                    diag_of[ki] = len(diag_of)
                kb_list.append((it, kb, qlo, w, diag))
                prevpar.append(lastpar[kb & 1])
                lastpar[kb & 1] = ki
                prevpt.append(lastpt[kb & 3])
                lastpt[kb & 3] = ki
        ki_of = {}
        k0 = 0
        for it, (pj, qc, kbs) in enumerate(iters):
            ki_of[it] = k0
            k0 += len(kbs)

        bs = dict(cnt)
        def w1_chunk(i):
            return i // 2, (i % 2) * 2048

        @blk.sync
        def _(sy):
            sy = Rec(sy)
            bwait(sy)
            for i in range(16):
                dk, co = w1_chunk(i)
                if i >= 2:
                    sy.wait_ge(sm["b2"], bs["b2"] + i - 1)
                sy.dma_start(out=wbuf[i % 2][:],
                             in_=d_w1T[dk * 128:(dk + 1) * 128, co:co + 2048]).then_inc(sm["dw"], 16)
                cnt["dw"] += 16
            for i in range(16):
                dk, co = w1_chunk(i)
                if i < 2:
                    sy.wait_ge(sm["b2"], bs["b2"] + 16)
                else:
                    sy.wait_ge(sm["b7"], bs["b7"] + i - 1)
                sy.dma_start(out=wbuf[i % 2][:],
                             in_=d_w1T[dk * 128:(dk + 1) * 128, co:co + 2048]).then_inc(sm["dw"], 16)
                cnt["dw"] += 16
            sy.wait_ge(sm["dw"], cnt["dw"])

        @blk.tensor
        def _(te):
            te = Rec(te)
            bwait(te)
            def emit_S(ki):
                it, kb, qlo, w, diag = kb_list[ki]
                pj = iters[it][0]
                pv = prevpar[ki]
                if pv is not None:
                    te.wait_ge(sm["a2"], bs["a2"] + pv + 1)
                te.matmul(ps_S[0][kb % 2][:, 0:w],
                          g["KT"][0:64, pj, kb * 128:(kb + 1) * 128],
                          g["QT"][0:64, pj, qlo:qlo + w],
                          start=True, stop=True, tile_position=(0, 0))
                te.matmul(ps_S[1][kb % 2][:, 0:w],
                          g["KT"][64:128, pj, kb * 128:(kb + 1) * 128],
                          g["QT"][64:128, pj, qlo:qlo + w],
                          start=True, stop=True, tile_position=(64, 0))
                incl(te, "a1")

            for it, (pj, qc, kbs) in enumerate(iters):
                kib = ki_of[it]
                for j, kb in enumerate(kbs):
                    ki = kib + j
                    _, _, qlo, w, diag = kb_list[ki]
                    if j == 0:
                        emit_S(ki)
                        if len(kbs) > 1:
                            emit_S(ki + 1)
                    te.wait_ge(sm["a2"], bs["a2"] + ki + 1)
                    if diag:
                        te.wait_ge(sm["a3"], bs["a3"] + diag_of[ki] + 1)
                    if j == 0 and it >= 1:
                        te.wait_ge(sm["a5"], bs["a5"] + 2 * it)
                    dst = qlo - qc * QCH
                    last = (j == len(kbs) - 1)
                    te.matmul(ps_oA[:, dst:dst + w], g["VA"][:, kb, 2 * pj, :],
                              PT[0][kb % 4][:, 0:w], start=(j == 0), stop=last)
                    incl(te, "a4")
                    te.matmul(ps_oB[:, dst:dst + w], g["VA"][:, kb, 2 * pj + 1, :],
                              PT[1][kb % 4][:, 0:w], start=(j == 0), stop=last)
                    incl(te, "a4")
                    if j + 2 < len(kbs):
                        emit_S(kib + j + 2)
                if it >= 1:
                    itp = it - 1
                    for a_ in range(2):
                        m = 2 * itp + a_
                        te.wait_ge(sm["a6"], bs["a6"] + m + 1)
                        if m >= 1:
                            te.wait_ge(sm["a8"], bs["a8"] + m)
                        te.matmul(ps_bc[:], g["ones_f"][64:65, 0:64], rcp[a_][itp % 2][64:65, :],
                                  start=True, stop=True, tile_position=(64, 0))
                        incl(te, "a7")
            itp = len(iters) - 1
            for a_ in range(2):
                m = 2 * itp + a_
                te.wait_ge(sm["a6"], bs["a6"] + m + 1)
                te.wait_ge(sm["a8"], bs["a8"] + m)
                te.matmul(ps_bc[:], g["ones_f"][64:65, 0:64], rcp[a_][itp % 2][64:65, :],
                          start=True, stop=True, tile_position=(64, 0))
                incl(te, "a7")
            te.wait_ge(sm["b5"], bs["b5"] + 1)
            te.matmul(ps_ws[0:1, 0:1], wsum[:], g["ones_f"][:, 0:1], start=True, stop=True)
            incl(te, "c3")

        @blk.tensor
        def _(te):
            te = Rec(te)
            te.wait_ge(sm["c4"], bs["c4"] + 1)
            te.matmul(ps_ws[:, 1:2], g["ones_f"][0:1, 0:128], tsum[:], start=True, stop=True)
            incl(te, "c5")

        @blk.scalar
        def _(a):
            a = Rec(a)
            bwait(a)
            for it, (pj, qc, kbs) in enumerate(iters):
                kib = ki_of[it]
                for j, kb in enumerate(kbs):
                    ki = kib + j
                    a.wait_ge(sm["a1"], bs["a1"] + ki + 1)
                    pv = prevpt[ki]
                    if pv is not None:
                        a.wait_ge(sm["a4"], bs["a4"] + 2 * (pv + 1))
                    _, _, qlo, w, diag = kb_list[ki]
                    a.activation(PT[0][kb % 4][:, 0:w], ps_S[0][kb % 2][:, 0:w],
                                 AF.Exp, bias=g["negc"][:, 2 * pj:2 * pj + 1], scale=0.125)
                    a.activation(PT[1][kb % 4][:, 0:w], ps_S[1][kb % 2][:, 0:w],
                                 AF.Exp, bias=g["negc"][:, 2 * pj + 1:2 * pj + 2], scale=0.125)
                    incl(a, "a2", drain=True)

        @blk.vector
        def _(v):
            v = Rec(v)
            bwait(v)
            for it, (pj, qc, kbs) in enumerate(iters):
                kib = ki_of[it]
                for j, kb in enumerate(kbs):
                    ki = kib + j
                    _, _, qlo, w, diag = kb_list[ki]
                    if diag:
                        v.wait_ge(sm["a2"], bs["a2"] + ki + 1)
                        v.tensor_tensor(PT[0][kb % 4][:, 0:128], PT[0][kb % 4][:, 0:128],
                                        g["mask"][:], ALU.mult)
                        v.tensor_tensor(PT[1][kb % 4][:, 0:128], PT[1][kb % 4][:, 0:128],
                                        g["mask"][:], ALU.mult)
                        incl(v, "a3", drain=True)
                ki_last = kib + len(kbs) - 1
                v.wait_ge(sm["a4"], bs["a4"] + 2 * (ki_last + 1))
                for a_, ps_o in ((0, ps_oA), (1, ps_oB)):
                    ua = uaT[a_][it % 2]
                    v.tensor_copy(ua[:], ps_o[:])
                    v.tensor_scalar(ua[64:65, :], ua[64:65, :], SUM_FLOOR, None, ALU.max)
                    incl(v, "a5", drain=True)
                    if it >= 2:
                        v.wait_ge(sm["a7"], bs["a7"] + 2 * (it - 2) + a_ + 1)
                    v.reciprocal(rcp[a_][it % 2][64:65, :], ua[64:65, :])
                    incl(v, "a6", drain=True)
                if it >= 1:
                    itp = it - 1
                    for a_ in range(2):
                        m = 2 * itp + a_
                        v.wait_ge(sm["a7"], bs["a7"] + m + 1)
                        if m >= 1:
                            v.wait_ge(sm["dcc"], bs["dcc"] + 16 * m)
                        v.tensor_tensor(stg[a_][:], uaT[a_][itp % 2][0:64, :], ps_bc[:], ALU.mult)
                        incl(v, "a8", drain=True)
                if it < 16:
                    v.wait_ge(sm["dw"], bs["dw"] + 16 * min(it + 2, 16))
                    v.tensor_reduce(wpart[:], wbuf[it % 2][:], AX.X, ALU.add,
                                    apply_absolute_value=True)
                    v.drain()
                    if it == 0:
                        v.tensor_copy(wsum[:], wpart[:])
                    else:
                        v.tensor_add(wsum[:], wsum[:], wpart[:])
                    incl(v, "b2", drain=True)
                    if it == 15:
                        incl(v, "b5")
            itp = len(iters) - 1
            for a_ in range(2):
                m = 2 * itp + a_
                v.wait_ge(sm["a7"], bs["a7"] + m + 1)
                v.wait_ge(sm["dcc"], bs["dcc"] + 16 * m)
                v.tensor_tensor(stg[a_][:], uaT[a_][itp % 2][0:64, :], ps_bc[:], ALU.mult)
                incl(v, "a8", drain=True)

        @blk.vector
        def _(v):
            v = Rec(v)
            v.wait_ge(sm["c3"], bs["c3"] + 1)
            v.tensor_copy(tsum[:], ps_ws[0:1, 0:1])
            incl(v, "c4", drain=True)
            v.wait_ge(sm["c5"], bs["c5"] + 1)
            v.tensor_scalar(g["mw1"][:], ps_ws[:, 1:2], 1.0 / (C * DH), CLIP, ALU.mult, ALU.max)
            v.drain()
            v.reciprocal(g["sw1"][:], g["mw1"][:])
            incl(v, "b6", drain=True)
            for ii in range(16):
                dk, co = w1_chunk(ii)
                v.wait_ge(sm["b7"], bs["b7"] + ii + 1)
                v.tensor_scalar(utmp[:], utmp[:], MAGIC, 1.0, ALU.subtract, ALU.min)
                v.tensor_scalar(g["w1qT"][:, dk, co:co + 2048], utmp[:], -1.0, None, ALU.max)
                incl(v, "b3", drain=True)

        @blk.scalar
        def _(a):
            a = Rec(a)
            for ii in range(16):
                a.wait_ge(sm["b6"], bs["b6"] + 1)
                a.wait_ge(sm["dw"], bs["dw"] + 16 * min(16 + ii + 2, 32))
                if ii >= 1:
                    a.wait_ge(sm["b3"], bs["b3"] + ii)
                a.activation(utmp[:], wbuf[ii % 2][:], AF.Identity, bias=g["magic"][:], scale=g["sw1"][:])
                incl(a, "b7", drain=True)

        @blk.gpsimd
        def _(gp):
            gp = Rec(gp)
            bwait(gp)
            for it in range(len(iters)):
                pj, qc, kbs = iters[it]
                half, qs = qc // (NQC // 2), (qc % (NQC // 2)) * QCH
                for a_ in range(2):
                    m = 2 * it + a_
                    gp.wait_ge(sm["a8"], bs["a8"] + m + 1)
                    gp.dma_start(out=d_ccin[half, pj, 64 * a_:64 * a_ + 64, qs:qs + QCH],
                                 in_=stg[a_][:]).then_inc(sm["dcc"], 16)
                    cnt["dcc"] += 16
                if it == len(iters) // 2 - 1:
                    gp.wait_ge(sm["dcc"], bs["dcc"] + 16 * (2 * it + 2))
                    gp.collective_compute(
                        "AllGather", ALU.bypass,
                        replica_groups=[[0, 1], [2, 3], [4, 5], [6, 7]],
                        ins=[d_ccin[0].opt()], outs=[d_ccoutA.ap().opt()],
                    ).then_inc(sm["c2"], 1)
                    cnt["c2"] += 1
            gp.wait_ge(sm["dcc"], cnt["dcc"])
            gp.collective_compute(
                "AllGather", ALU.bypass,
                replica_groups=[[0, 1], [2, 3], [4, 5], [6, 7]],
                ins=[d_ccin[1].opt()], outs=[d_ccoutB.ap().opt()],
            ).then_inc(sm["c2"], 1)
            cnt["c2"] += 1
            gp.wait_ge(sm["c2"], cnt["c2"])

        if "attn" in dbg:
            @blk.gpsimd
            def _(gp):
                gp = Rec(gp)
                gp.dma_start(out=dbg_t["ccin"].ap(), in_=d_ccin.ap()).then_inc(sm["dbg"], 16)
                cnt["dbg"] += 16
                gp.wait_ge(sm["dbg"], cnt["dbg"])

        barrier_close()
        pp.close()
        p2.close()

    ph2()
    stB.close()
    if phases < 3:
        st.close()
        return nc
    g["w2qT"] = st.enter_context(nc.sbuf_tensor("w2qT", [128, 32, C], FP8, side="right"))

    def ph26():
        p3 = ExitStack()
        def p3sb(name, shape, dt):
            return p3.enter_context(nc.sbuf_tensor(name, shape, dt))
        woT = p3sb("woT_s", [128, 16, C], BF16)
        gath = p3sb("gath", [128, 16, OWN], BF16)
        ob = [p3sb(f"ob{i}", [128, C], F32) for i in range(2)]
        wbuf = [p3sb(f"w2b{i}", [128, 2048], F32) for i in range(2)]
        utmp = p3sb("utmp2", [128, 2048], F32)
        wsum = p3sb("wsum2", [128, 1], F32)
        wpart = p3sb("wpart2", [128, 1], F32)
        tsum = p3sb("tsum2", [1, 1], F32)

        pp = ExitStack()
        ps_o = [pp.enter_context(nc.psum_tensor(f"ps_op{i}", [128, C], F32)) for i in range(2)]
        ps_ws = pp.enter_context(nc.psum_tensor("ps_ws2", [128, 2], F32))

        bs = dict(cnt)

        @blk.sync
        def _(sy):
            sy = Rec(sy)
            bwait(sy)
            for i in range(18):
                if 2 <= i < 16:
                    sy.wait_ge(sm["b2"], bs["b2"] + i - 1)
                if i < 16:
                    sy.dma_start(out=wbuf[i % 2][:].rearrange("p (a q) -> p a q", a=2),
                                 in_=d_w2T[i * 256:(i + 1) * 256, :].rearrange("(a p) q -> p a q", p=128)).then_inc(sm["dw"], 16)
                    cnt["dw"] += 16
                if i >= 2:
                    tt = i - 2
                    sy.wait_ge(sm["a3"], bs["a3"] + tt + 1)
                    sy.dma_start(out=d_atny[tt * 128:(tt + 1) * 128, :], in_=ob[tt % 2][:]).then_inc(sm["day"], 16)
                    cnt["day"] += 16
            for i in range(16):
                if i < 2:
                    sy.wait_ge(sm["b2"], bs["b2"] + 16)
                else:
                    sy.wait_ge(sm["b7"], bs["b7"] + i - 1)
                sy.dma_start(out=wbuf[i % 2][:].rearrange("p (a q) -> p a q", a=2),
                             in_=d_w2T[i * 256:(i + 1) * 256, :].rearrange("(a p) q -> p a q", p=128)).then_inc(sm["dw"], 16)
                cnt["dw"] += 16
            sy.wait_ge(sm["dw"], cnt["dw"])
            sy.wait_ge(sm["day"], cnt["day"])

        @blk.gpsimd
        def _(gp):
            gp = Rec(gp)
            bwait(gp)
            for kt in range(16):
                gp.dma_start(out=woT[:, kt, :], in_=d_woT[kt * 128:(kt + 1) * 128, :]).then_inc(sm["dg1"], 16)
                cnt["dg1"] += 16
            for kt in range(16):
                r_, h_, b_ = kt // 8, (kt // 4) % 2, kt % 4
                src_cc = d_ccoutA if h_ == 0 else d_ccoutB
                gp.dma_start(out=gath[:, kt, :],
                             in_=src_cc[r_, b_]).then_inc(sm["dcc"], 16)
                cnt["dcc"] += 16
            gp.wait_ge(sm["dcc"], cnt["dcc"])
            gp.wait_ge(sm["dg1"], cnt["dg1"])


        dc_woT = cnt["dg1"]

        @blk.tensor
        def _(te):
            te = Rec(te)
            bwait(te)
            te.wait_ge(sm["dg1"], dc_woT)
            te.wait_ge(sm["dcc"], bs["dcc"] + 16 * 16)
            for tt in range(16):
                if tt >= 2:
                    te.wait_ge(sm["a3"], bs["a3"] + tt - 1)
                for kt in range(16):
                    for n in range(2):
                        te.matmul(ps_o[tt % 2][:, n * 512:(n + 1) * 512],
                                  gath[:, kt, tt * 128:(tt + 1) * 128],
                                  woT[:, kt, n * 512:(n + 1) * 512],
                                  start=(kt == 0), stop=(kt == 15))
                incl(te, "a2")
            te.wait_ge(sm["b5"], bs["b5"] + 1)
            te.matmul(ps_ws[0:1, 0:1], wsum[:], g["ones_f"][:, 0:1], start=True, stop=True)
            incl(te, "c3")

        @blk.tensor
        def _(te):
            te = Rec(te)
            te.wait_ge(sm["c4"], bs["c4"] + 1)
            te.matmul(ps_ws[:, 1:2], g["ones_f"][0:1, 0:128], tsum[:], start=True, stop=True)
            incl(te, "c5")

        @blk.vector
        def _(v):
            v = Rec(v)
            bwait(v)
            for tt in range(16):
                v.wait_ge(sm["dw"], bs["dw"] + 16 * min(tt + 2, 16))
                v.tensor_reduce(wpart[:], wbuf[tt % 2][:], AX.X, ALU.add,
                                apply_absolute_value=True)
                v.drain()
                if tt == 0:
                    v.tensor_copy(wsum[:], wpart[:])
                else:
                    v.tensor_add(wsum[:], wsum[:], wpart[:])
                incl(v, "b2", drain=True)
                if tt == 15:
                    incl(v, "b5")
                v.wait_ge(sm["a2"], bs["a2"] + tt + 1)
                if tt >= 2:
                    v.wait_ge(sm["day"], bs["day"] + 16 * tt)
                v.tensor_tensor(ob[tt % 2][:, 0:512], ps_o[tt % 2][:, 0:512],
                                g["bo_bc_s"][:, 0:512], ALU.add)
                v.tensor_tensor(ob[tt % 2][:, 512:1024], ps_o[tt % 2][:, 512:1024],
                                g["bo_bc_s"][:, 512:1024], ALU.add)
                incl(v, "a3", drain=True)
            v.wait_ge(sm["c3"], bs["c3"] + 1)
            v.tensor_copy(tsum[:], ps_ws[0:1, 0:1])
            incl(v, "c4", drain=True)
            v.wait_ge(sm["c5"], bs["c5"] + 1)
            v.tensor_scalar(g["mw2"][:], ps_ws[:, 1:2], 1.0 / (C * DH), CLIP, ALU.mult, ALU.max)
            v.drain()
            v.reciprocal(g["sw2"][:], g["mw2"][:])
            incl(v, "b6", drain=True)
            for i in range(16):
                v.wait_ge(sm["b7"], bs["b7"] + i + 1)
                v.tensor_scalar(utmp[:], utmp[:], MAGIC, 1.0, ALU.subtract, ALU.min)
                for a_ in range(2):
                    v.tensor_scalar(g["w2qT"][:, 2 * i + a_, :], utmp[:, a_ * 1024:(a_ + 1) * 1024],
                                    -1.0, None, ALU.max)
                incl(v, "b3", drain=True)

        @blk.scalar
        def _(a):
            a = Rec(a)
            bwait(a)
            for i in range(16):
                a.wait_ge(sm["b6"], bs["b6"] + 1)
                a.wait_ge(sm["dw"], bs["dw"] + 16 * min(16 + i + 2, 32))
                if i >= 1:
                    a.wait_ge(sm["b3"], bs["b3"] + i)
                a.activation(utmp[:], wbuf[i % 2][:], AF.Identity, bias=g["magic"][:], scale=g["sw2"][:])
                incl(a, "b7", drain=True)

        if "oproj" in dbg:
            @blk.gpsimd
            def _(gp):
                gp = Rec(gp)
                gp.wait_ge(sm["day"], cnt["day"])
                gp.dma_start(out=dbg_t["atny"].ap(), in_=d_atny.ap()).then_inc(sm["dbg"], 16)
                cnt["dbg"] += 16
                gp.wait_ge(sm["dbg"], cnt["dbg"])

        barrier_close()
        pp.close()
        p3.close()

    ph26()
    if phases < 4:
        st.close()
        return nc

    def ph3():
        p4 = ExitStack()
        def p4sb(name, shape, dt):
            return p4.enter_context(nc.sbuf_tensor(name, shape, dt))
        xf = [p4sb(f"xf{i}", [128, C], F32) for i in range(2)]
        nx = [p4sb(f"nx{i}", [128, C], BF16) for i in range(2)]
        nxT = [p4sb(f"nxT{i}", [128, 8, FCH], BF16) for i in range(2)]
        hwin = [p4sb(f"hwin{i}", [128, 32, FCH], F32) for i in range(2)]
        cvo = [p4sb(f"cvo{i}", [128, 32, FCH], F32) for i in range(2)]
        hq = [p4sb(f"hq{i}", [128, 32, FCH], BF16) for i in range(2)]
        sigT = [p4sb(f"sigT{i}", [128, FCH], F32) for i in range(2)]
        sgtm = [p4sb(f"sgtm{i}", [128, 128], F32) for i in range(2)]
        Mab = p4sb("Mab", [128, FCH], F32)
        s2b = p4sb("s2b", [128, FCH], F32)
        s2tmb = p4sb("s2tmb", [128, 128], F32)
        ma2 = p4sb("ma2", [128, 1], F32)
        f2b = [p4sb(f"f2{i}", [128, 1], F32) for i in range(2)]
        ayb = p4sb("ayb", [128, C], F32)
        xrb = p4sb("xrb", [128, C], F32)
        yb = [p4sb(f"yb{i}", [128, C], F32) for i in range(2)]
        sqp = [p4sb(f"sqp{i}", [128, C], F32) for i in range(2)]
        sumsB = p4sb("sumsB", [128, NCH], F32)
        sumsqB = p4sb("sumsqB", [128, NCH], F32)
        mxB = p4sb("mxB", [128, NCH], F32)
        mnB = p4sb("mnB", [128, NCH], F32)
        mvB = p4sb("mvB", [128, NCH], F32)
        vepsB = p4sb("vepsB", [128, NCH], F32)
        lnvB = p4sb("lnvB", [128, NCH], F32)
        rsB = p4sb("rsB", [128, NCH], F32)
        macB = p4sb("macB", [128, NCH], F32)
        sclB = p4sb("sclB", [128, NCH], F32)
        biaB = p4sb("biaB", [128, NCH], F32)
        fmacB = p4sb("fmacB", [128, NCH], F32)
        strip = p4sb("strip", [128, 32, 4], F32)
        tms = p4sb("tms", [128, 32, 2], F32)

        pp = ExitStack()
        ps_m1 = [pp.enter_context(nc.psum_tensor(f"ps_m1{i}", [128, 1024], F32)) for i in range(2)]
        ps_m2 = pp.enter_context(nc.psum_tensor("ps_m2", [128, C], F32))
        ps_tr = pp.enter_context(nc.psum_tensor("ps_tr3", [128, 128], F32))

        bs = dict(cnt)

        @blk.sync
        def _(sy):
            sy = Rec(sy)
            bwait(sy)
            for c in range(0, NCH, 2):
                if c >= 2:
                    sy.wait_ge(sm["b1"], bs["b1"] + c - 1)
                r0 = 127 + FCH * c
                sy.dma_start(out=xf[c % 2][:], in_=d_xffn[r0:r0 + 128, :]).then_inc(sm["dc"], 16)
                cnt["dc"] += 16
            for c in range(NCH):
                if c < 2:
                    sy.wait_ge(sm["b1"], bs["b1"] + NCH)
                else:
                    sy.wait_ge(sm["a5"], bs["a5"] + c - 1)
                r0 = 127 + FCH * c
                sy.dma_start(out=xf[c % 2][:], in_=d_xffn[r0:r0 + 128, :]).then_inc(sm["df"], 16)
                cnt["df"] += 16
                if c >= 1:
                    cc = c - 1
                    sy.wait_ge(sm["c4"], bs["c4"] + cc)
                    r0 = 128 + FCH * cc
                    sy.dma_start(out=ayb[:], in_=d_atny[r0 - 128:r0, :]).then_inc(sm["day"], 16)
                    cnt["day"] += 16
                    sy.dma_start(out=xrb[:], in_=d_xffn[r0:r0 + 128, :]).then_inc(sm["dres"], 16)
                    cnt["dres"] += 16
                    if cc >= 1:
                        sy.dma_start(out=d_y[FCH * (cc - 1):FCH * cc, :],
                                     in_=yb[(cc - 1) % 2][:]).then_inc(sm["dy"], 16)
                        cnt["dy"] += 16
            sy.wait_ge(sm["c4"], bs["c4"] + NFC)
            sy.dma_start(out=d_y[OWN - FCH:OWN, :], in_=yb[(NFC - 1) % 2][:]).then_inc(sm["dy"], 16)
            cnt["dy"] += 16
            sy.wait_ge(sm["dy"], cnt["dy"])

        @blk.vector
        def _(v):
            v = Rec(v)
            bwait(v)
            for c in range(NCH):
                if c % 2 == 0:
                    v.wait_ge(sm["dc"], bs["dc"] + 16 * (c // 2 + 1))
                else:
                    v.wait_ge(sm["dg1"], bs["dg1"] + 16 * ((c + 1) // 2))
                v.reduce_sum(sumsB[:, c:c + 1], xf[c % 2][:], AX.X)
                v.reduce_max(mxB[:, c:c + 1], xf[c % 2][:], AX.X)
                v.tensor_reduce(mnB[:, c:c + 1], xf[c % 2][:], AX.X, ALU.min)
                v.wait_ge(sm["p1"], bs["p1"] + c + 1)
                v.reduce_sum(sumsqB[:, c:c + 1], sqp[c % 2][:], AX.X)
                incl(v, "b1", drain=True)
            v.tensor_scalar(mvB[:], sumsB[:], 1.0 / C, None, ALU.mult)
            v.drain()
            v.tensor_tensor(vepsB[:], mvB[:], mvB[:], ALU.mult)
            v.drain()
            v.scalar_tensor_tensor(vepsB[:], sumsqB[:], 1.0 / C, vepsB[:],
                                   ALU.mult, ALU.subtract)
            v.drain()
            v.tensor_scalar(vepsB[:], vepsB[:], EPS, None, ALU.add)
            incl(v, "a1", drain=True)
            v.wait_ge(sm["p1"], bs["p1"] + NCH + 1)
            v.tensor_tensor(mxB[:], mxB[:], mvB[:], ALU.subtract)
            v.tensor_tensor(mnB[:], mvB[:], mnB[:], ALU.subtract)
            v.drain()
            v.tensor_tensor(macB[:], mxB[:], mnB[:], ALU.max)
            v.drain()
            v.tensor_tensor(macB[:], macB[:], rsB[:], ALU.mult)
            v.drain()
            v.tensor_scalar(macB[:], macB[:], CLIP, None, ALU.max)
            v.drain()
            v.reciprocal(sclB[:], macB[:])
            v.drain()
            v.tensor_scalar(sclB[:], sclB[:], 127.0, None, ALU.mult)
            v.drain()
            v.tensor_tensor(sclB[:], sclB[:], rsB[:], ALU.mult)
            v.drain()
            v.scalar_tensor_tensor(biaB[:], mvB[:], -1.0, sclB[:], ALU.mult, ALU.mult)
            v.drain()
            v.tensor_scalar(biaB[:], biaB[:], MAGIC, None, ALU.add)
            v.tensor_tensor(fmacB[:], macB[:], g["mw1"][:].broadcast_to([128, NCH]), ALU.mult)
            v.drain()
            v.tensor_scalar(fmacB[:], fmacB[:], 1.0 / 127.0, None, ALU.mult)
            v.drain()
            def yb_evac(cc):
                v.wait_ge(sm["c3"], bs["c3"] + cc + 1)
                v.wait_ge(sm["day"], bs["day"] + 16 * (cc + 1))
                v.wait_ge(sm["dres"], bs["dres"] + 16 * (cc + 1))
                if cc >= 2:
                    v.wait_ge(sm["dy"], bs["dy"] + 16 * cc)
                for hb in range(2):
                    sl = slice(hb * 512, (hb + 1) * 512)
                    v.scalar_tensor_tensor(yb[cc % 2][:, sl], ps_m2[:, sl], f2b[cc % 2][:],
                                           ayb[:, sl], ALU.mult, ALU.add)
                    v.tensor_add(yb[cc % 2][:, sl], yb[cc % 2][:, sl], xrb[:, sl])
                incl(v, "c4", drain=True)

            for c in range(NCH):
                if c >= 2:
                    yb_evac(c - 2)
                v.tensor_scalar(sgtm[c % 2][:], g["ones_f"][:], fmacB[:, c:c + 1], None, ALU.mult)
                incl(v, "a3", drain=True)
                v.wait_ge(sm["a4"], bs["a4"] + c + 1)
                v.tensor_scalar(nx[c % 2][:], xf[c % 2][:], MAGIC, None, ALU.subtract)
                incl(v, "a5", drain=True)
                v.wait_ge(sm["a6"], bs["a6"] + 2 * c + 1)
                for b_ in range(8):
                    v.tensor_copy(nxT[c % 2][:, b_, :], ps_m1[0][:, b_ * 128:(b_ + 1) * 128])
                v.wait_ge(sm["a6"], bs["a6"] + 2 * c + 2)
                v.tensor_copy(sigT[c % 2][:], ps_tr[:])
                incl(v, "a7", drain=True)
                if c >= 1:
                    cc = c - 1
                    hc = hwin[cc % 2]
                    cvc = cvo[cc % 2]
                    v.wait_ge(sm["a1"], bs["a1"] + cc + 2)

                def conv_batch(pts):
                    for pt in pts:
                        w1_ = g["cwb"][:, 32 + pt:32 + pt + 1]
                        w2_ = g["cwb"][:, 64 + pt:64 + pt + 1]
                        v.scalar_tensor_tensor(cvc[:, pt, 0:FCH - 2], hc[:, pt, 1:FCH - 1], w1_,
                                               cvc[:, pt, 0:FCH - 2], ALU.mult, ALU.add)
                        v.scalar_tensor_tensor(cvc[:, pt, 0:FCH - 2], hc[:, pt, 2:FCH], w2_,
                                               cvc[:, pt, 0:FCH - 2], ALU.mult, ALU.add)

                for g4 in range(4):
                    if c >= 1:
                        conv_batch(range(8 * g4, 8 * g4 + 8))
                    v.wait_ge(sm["a8"], bs["a8"] + 4 * c + g4 + 1)
                    srep = sigT[c % 2][:].unsqueeze(1).broadcast_to([128, 4, FCH])
                    for hb in range(2):
                        v.tensor_tensor(hwin[c % 2][:, 8 * g4 + 4 * hb:8 * g4 + 4 * hb + 4, :],
                                        ps_m1[g4 % 2][:, hb * 512:(hb + 1) * 512].rearrange("p (a q) -> p a q", a=4),
                                        srep, ALU.mult)
                    incl(v, "b2", drain=True)
                if c >= 1:
                    v.tensor_copy(strip[:, :, 0:2], hc[:, :, FCH - 2:FCH])
                    v.tensor_copy(strip[:, :, 2:4], hwin[c % 2][:, :, 0:2])
                    v.drain()
                    w0s = g["cwb"][:, 0:32].unsqueeze(2).broadcast_to([128, 32, 2])
                    w1s = g["cwb"][:, 32:64].unsqueeze(2).broadcast_to([128, 32, 2])
                    w2s = g["cwb"][:, 64:96].unsqueeze(2).broadcast_to([128, 32, 2])
                    bbs = g["cwb"][:, 96:128].unsqueeze(2).broadcast_to([128, 32, 2])
                    cvs = cvc[:, :, FCH - 2:FCH]
                    v.tensor_tensor(cvs, strip[:, :, 0:2], w0s, ALU.mult)
                    v.tensor_tensor(tms[:], strip[:, :, 1:3], w1s, ALU.mult)
                    v.drain()
                    v.tensor_add(cvs, cvs, tms[:])
                    v.tensor_tensor(tms[:], strip[:, :, 2:4], w2s, ALU.mult)
                    v.drain()
                    v.tensor_add(cvs, cvs, tms[:])
                    v.drain()
                    v.tensor_add(cvs, cvs, bbs)
                    incl(v, "b3", drain=True)
                    v.wait_ge(sm["b4"], bs["b4"] + cc + 1)
                    v.tensor_reduce(Mab[:].unsqueeze(2), cvc[:].transpose([0, 2, 1]),
                                    AX.X, ALU.max, apply_absolute_value=True)
                    incl(v, "b5", drain=True)
                    v.wait_ge(sm["c1"], bs["c1"] + cc + 1)
                    v.reduce_max(ma2[:], ps_tr[:], AX.X)
                    v.drain()
                    v.tensor_scalar(ma2[:], ma2[:], CLIP, None, ALU.max)
                    v.drain()
                    v.tensor_scalar(f2b[cc % 2][:], ma2[:], g["mw2"][:], 1.0 / 127.0, ALU.mult, ALU.mult)
                    v.reciprocal(ma2[:], ma2[:])
                    v.drain()
                    v.tensor_scalar(ma2[:], ma2[:], 127.0, None, ALU.mult)
                    v.drain()
                    v.tensor_scalar(s2tmb[:], g["ones_f"][:], ma2[:], None, ALU.mult)
                    incl(v, "b6", drain=True)
                    v.wait_ge(sm["c5"], bs["c5"] + cc + 1)
                    v.tensor_copy(s2b[:], ps_tr[:])
                    s2rep = s2b[:].unsqueeze(1).broadcast_to([128, 32, FCH])
                    if cc >= 2:
                        v.wait_ge(sm["c3"], bs["c3"] + cc - 1)
                    v.tensor_tensor(cvc[:], cvc[:], s2rep, ALU.mult)
                    v.tensor_scalar(hq[cc % 2][:], cvc[:], MAGIC, MAGIC, ALU.add, ALU.subtract)
                    incl(v, "b8", drain=True)
            yb_evac(NFC - 1)

        @blk.scalar
        def _(a):
            a = Rec(a)
            bwait(a)
            for c in range(NCH):
                if c % 2 == 0:
                    a.wait_ge(sm["dc"], bs["dc"] + 16 * (c // 2 + 1))
                else:
                    a.wait_ge(sm["dg1"], bs["dg1"] + 16 * ((c + 1) // 2))
                if c >= 2:
                    a.wait_ge(sm["b1"], bs["b1"] + c - 1)
                a.activation(sqp[c % 2][:], xf[c % 2][:], AF.Square)
                incl(a, "p1", drain=True)
            a.wait_ge(sm["a1"], bs["a1"] + 1)
            a.activation(lnvB[:], vepsB[:], AF.Ln)
            a.drain()
            a.activation(rsB[:], lnvB[:], AF.Exp, scale=-0.5)
            incl(a, "p1", drain=True)
            for c in range(NCH):
                if c >= 1:
                    cc = c - 1
                    a.wait_ge(sm["b2"], bs["b2"] + 4 * (cc + 1))
                    if cc >= 2:
                        a.wait_ge(sm["b8"], bs["b8"] + cc - 1)
                    cvc = cvo[cc % 2]
                    hc = hwin[cc % 2]
                    for pt in range(32):
                        a.activation(cvc[:, pt, 0:FCH - 2], hc[:, pt, 0:FCH - 2], AF.Identity,
                                     bias=g["cwb"][:, 96 + pt:96 + pt + 1],
                                     scale=g["cwb"][:, pt:pt + 1])
                    incl(a, "a1", drain=True)
                a.wait_ge(sm["df"], bs["df"] + 16 * min(c + 2, NCH))
                a.wait_ge(sm["a3"], bs["a3"] + c + 1)
                a.activation(xf[c % 2][:], xf[c % 2][:], AF.Identity,
                             bias=biaB[:, c:c + 1], scale=sclB[:, c:c + 1])
                incl(a, "a4", drain=True)
                if c >= 1:
                    cc = c - 1
                    a.wait_ge(sm["b3"], bs["b3"] + cc + 1)
                    a.activation(cvo[cc % 2][:], cvo[cc % 2][:], AF.Silu)
                    incl(a, "b4", drain=True)

        @blk.tensor
        def _(te):
            te = Rec(te)
            bwait(te)
            for c in range(NCH):
                te.wait_ge(sm["a5"], bs["a5"] + c + 1)
                te.wait_ge(sm["a7"], bs["a7"] + c)
                if c >= 2:
                    te.wait_ge(sm["b8"], bs["b8"] + c - 1)
                if c >= 1:
                    te.wait_ge(sm["b2"], bs["b2"] + 4 * c)
                for b_ in range(8):
                    te.matmul(ps_m1[0][:, b_ * 128:(b_ + 1) * 128],
                              nx[c % 2][:, b_ * 128:(b_ + 1) * 128], g["idb"][:],
                              start=True, stop=True)
                incl(te, "a6")
                te.matmul(ps_tr[:], sgtm[c % 2][:], g["idf"][:], start=True, stop=True)
                incl(te, "a6")
                te.wait_ge(sm["a7"], bs["a7"] + c + 1)
                for g4 in range(4):
                    if 4 * c + g4 >= 2:
                        te.wait_ge(sm["b2"], bs["b2"] + 4 * c + g4 - 1)
                    for htl in range(8):
                        ht = 8 * g4 + htl
                        for dk in range(8):
                            te.matmul(ps_m1[g4 % 2][:, htl * FCH:(htl + 1) * FCH],
                                      g["w1qT"][:, dk, ht * 128:(ht + 1) * 128],
                                      nxT[c % 2][:, dk, :],
                                      start=(dk == 0), stop=(dk == 7))
                    incl(te, "a8")
                if c >= 1:
                    cc = c - 1
                    te.wait_ge(sm["b5"], bs["b5"] + cc + 1)
                    te.matmul(ps_tr[:], Mab[:], g["idf"][:], start=True, stop=True)
                    incl(te, "c1")
                    te.wait_ge(sm["b6"], bs["b6"] + cc + 1)
                    te.matmul(ps_tr[:], s2tmb[:], g["idf"][:], start=True, stop=True)
                    incl(te, "c5")
                    te.wait_ge(sm["b8"], bs["b8"] + cc + 1)
                    if cc >= 1:
                        te.wait_ge(sm["c4"], bs["c4"] + cc)
                    for kt in range(32):
                        for n in range(2):
                            te.matmul(ps_m2[:, n * 512:(n + 1) * 512],
                                      hq[cc % 2][:, kt, :],
                                      g["w2qT"][:, kt, n * 512:(n + 1) * 512],
                                      start=(kt == 0), stop=(kt == 31))
                    incl(te, "c3")

        @blk.gpsimd
        def _(gp):
            gp = Rec(gp)
            bwait(gp)
            for c in range(1, NCH, 2):
                gp.wait_ge(sm["b1"], bs["b1"] + c - 1)
                r0 = 127 + FCH * c
                gp.dma_start(out=xf[c % 2][:], in_=d_xffn[r0:r0 + 128, :]).then_inc(sm["dg1"], 16)
                cnt["dg1"] += 16
            gp.wait_ge(sm["dg1"], cnt["dg1"])

        if "ffn" in dbg:
            @blk.gpsimd
            def _(gp):
                gp = Rec(gp)
                gp.wait_ge(sm["dy"], cnt["dy"])
                gp.dma_start(out=dbg_t["w1q"].ap(), in_=g["w1qT"].ap()).then_inc(sm["dbg"], 16)
                gp.dma_start(out=dbg_t["w2q"].ap(), in_=g["w2qT"].ap()).then_inc(sm["dbg"], 16)
                gp.dma_start(out=dbg_t["h1"].ap(), in_=hwin[1][:]).then_inc(sm["dbg"], 16)
                gp.dma_start(out=dbg_t["hq0"].ap(), in_=hq[1][:]).then_inc(sm["dbg"], 16)
                cnt["dbg"] += 64
                gp.wait_ge(sm["dbg"], cnt["dbg"])

        barrier_close()
        pp.close()
        p4.close()

    ph3()
    st.close()
    return nc


def make_inputs(L0_attn_x, L0_attn_Wq, L0_attn_bq, L0_attn_Wk, L0_attn_bk,
                L0_attn_Wv, L0_attn_bv, L0_attn_Wo, L0_attn_bo,
                L0_ffn_w1, L0_ffn_conv_w, L0_ffn_conv_b, L0_ffn_w2):
    import ml_dtypes
    f32 = np.float32
    x = np.asarray(L0_attn_x, f32)
    Wq, Wk, Wv, Wo = (np.asarray(a, f32) for a in (L0_attn_Wq, L0_attn_Wk, L0_attn_Wv, L0_attn_Wo))
    bq, bk, bv, bo = (np.asarray(a, f32) for a in (L0_attn_bq, L0_attn_bk, L0_attn_bv, L0_attn_bo))
    w1 = np.asarray(L0_ffn_w1, f32)
    cw = np.asarray(L0_ffn_conv_w, f32).reshape(DH, 3)
    cb = np.asarray(L0_ffn_conv_b, f32)
    w2 = np.asarray(L0_ffn_w2, f32)

    theta = (10000.0 ** (-np.arange(0, D, 2).astype(f32) / D)).astype(f32)
    freqs = np.arange(T, dtype=f32)[:, None] * theta[None, :]
    cosv, sinv = np.cos(freqs).astype(f32), np.sin(freqs).astype(f32)
    ropeC = np.zeros((T, D), f32)
    ropeS = np.zeros((T, D), f32)
    ropeC[:, 0::2] = cosv; ropeC[:, 1::2] = cosv
    ropeS[:, 0::2] = -sinv; ropeS[:, 1::2] = sinv
    mask = (np.arange(128)[None, :] >= np.arange(128)[:, None]).astype(ml_dtypes.bfloat16)
    idb = np.eye(128, dtype=ml_dtypes.bfloat16)
    idf = np.eye(128, dtype=f32)
    cwb = np.zeros((128, 128), f32)
    for pt in range(32):
        cwb[:, pt] = cw[pt * 128:(pt + 1) * 128, 0]
        cwb[:, 32 + pt] = cw[pt * 128:(pt + 1) * 128, 1]
        cwb[:, 64 + pt] = cw[pt * 128:(pt + 1) * 128, 2]
        cwb[:, 96 + pt] = cb[pt * 128:(pt + 1) * 128]

    w1T = np.ascontiguousarray(w1.T)
    w2T = np.ascontiguousarray(w2.T)
    WoT = np.ascontiguousarray(Wo.T)
    in_maps = []
    for c in range(8):
        b, g_ = c // 2, c % 2
        s0 = g_ * OWN
        xffn = np.zeros((2432, C), f32)
        lo, hi = s0 - 128, s0 + 2304
        slo, shi = max(lo, 0), min(hi, T)
        xffn[slo - lo:shi - lo] = x[b, slo:shi]
        rows = slice(512 * g_, 512 * (g_ + 1))
        wqkvT = np.concatenate([Wq[rows].T, Wk[rows].T, Wv[rows].T], axis=1)
        bqkv = np.concatenate([bq[rows], bk[rows], bv[rows]])[None, :]
        woT = np.zeros((2048, C), f32)
        for r in range(2):
            for blk_ in range(4):
                kt = (r * 2 + g_) * 4 + blk_
                woT[kt * 128:(kt + 1) * 128] = WoT[512 * r + 128 * blk_:512 * r + 128 * (blk_ + 1)]
        in_maps.append({
            "x": np.ascontiguousarray(x[b]),
            "xffn": xffn,
            "wqkvT": np.ascontiguousarray(wqkvT),
            "bqkv": np.ascontiguousarray(bqkv),
            "woT": woT,
            "bo_bc": np.broadcast_to(bo[None, :], (128, C)).copy(),
            "w1T": w1T,
            "w2T": w2T,
            "convwb": cwb,
            "ropeC": np.ascontiguousarray(ropeC.reshape(NT, 128, D).transpose(1, 0, 2).reshape(128, NT * D)),
            "ropeS": np.ascontiguousarray(ropeS.reshape(NT, 128, D).transpose(1, 0, 2).reshape(128, NT * D)),
            "masktri": mask, "ident_bf": idb, "ident_f32": idf,
        })
    return in_maps


def kernel(**inputs):
    in_maps = make_inputs(**inputs)
    nc = build_program(phases=4)
    res = run_bass_kernel_spmd(nc, in_maps, core_ids=list(range(8)))
    y = np.zeros((B, T, C), np.float32)
    for c in range(8):
        b, g_ = c // 2, c % 2
        y[b, g_ * OWN:(g_ + 1) * OWN] = res.results[c]["y"]
    return y


# version 11 (final)
# speedup vs baseline: 1.0187x; 1.0187x over previous
import numpy as np
from contextlib import ExitStack

try:
    import concourse.bass as bass
except ImportError:
    import sys
    sys.path.insert(0, "/opt/trn_rl_repo")
    import concourse.bass as bass
import concourse.mybir as mybir
from concourse.bass_utils import run_bass_kernel_spmd
from concourse.bass_isa import ReduceOp
from concourse import library_config

F32 = mybir.dt.float32
BF16 = mybir.dt.bfloat16
FP8 = mybir.dt.float8e4
AF = mybir.ActivationFunctionType
ALU = mybir.AluOpType
AX = mybir.AxisListType

B, T, C, H, D, DH = 4, 4096, 1024, 16, 64, 4096
NT = T // 128
OWN = T // 2
MAGIC = float(np.float32(1.5 * 2 ** 23))
EPS = 1e-5
QCH = 512
NQC = T // QCH
FCH = 128
NFC = OWN // FCH
NCH = NFC + 1
CLIP = 1e-5
SUM_FLOOR = 1e-30

STAGE = 99

SEMNAMES = ("dx", "dc", "dw", "df", "dcc", "dy", "dres", "day",
            "a1", "a2", "a3", "a4", "a5", "a6", "a7", "a8",
            "b1", "b2", "b3", "b4", "b5", "b6", "b7", "b8",
            "c1", "c2", "c3", "c4", "c5", "ph", "dg1", "dbg", "p1", "p2", "p3")


def build_program(phases=4, dbg=()):
    nc = bass.Bass("TRN2", target_bir_lowering=False, debug=False,
                   detect_race_conditions=False)
    st = ExitStack()

    d_x = nc.dram_tensor("x", [T, C], F32, kind="ExternalInput")
    d_xffn = nc.dram_tensor("xffn", [2432, C], F32, kind="ExternalInput")
    d_wqkvT = nc.dram_tensor("wqkvT", [C, 1536], F32, kind="ExternalInput")
    d_bqkv = nc.dram_tensor("bqkv", [1, 1536], F32, kind="ExternalInput")
    d_woT = nc.dram_tensor("woT", [2048, C], F32, kind="ExternalInput")
    d_bo = nc.dram_tensor("bo_bc", [128, C], F32, kind="ExternalInput")
    d_w1T = nc.dram_tensor("w1T", [C, DH], F32, kind="ExternalInput")
    d_w2T = nc.dram_tensor("w2T", [DH, C], F32, kind="ExternalInput")
    d_cwb = nc.dram_tensor("convwb", [128, 128], F32, kind="ExternalInput")
    d_ropeC = nc.dram_tensor("ropeC", [128, NT * D], F32, kind="ExternalInput")
    d_ropeS = nc.dram_tensor("ropeS", [128, NT * D], F32, kind="ExternalInput")
    d_mask = nc.dram_tensor("masktri", [128, 128], BF16, kind="ExternalInput")
    d_idb = nc.dram_tensor("ident_bf", [128, 128], BF16, kind="ExternalInput")
    d_idf = nc.dram_tensor("ident_f32", [128, 128], F32, kind="ExternalInput")
    d_y = nc.dram_tensor("y", [OWN, C], F32, kind="ExternalOutput")
    d_ccin = nc.dram_tensor("cc_in", [2, 4, 128, OWN], BF16)
    d_ccoutA = nc.dram_tensor("cc_outA", [2, 4, 128, OWN], BF16)
    d_ccoutB = nc.dram_tensor("cc_outB", [2, 4, 128, OWN], BF16)
    d_atny = nc.dram_tensor("attnY", [OWN, C], F32)

    dbg_t = {}
    if "ln" in dbg:
        dbg_t["xnb0"] = nc.dram_tensor("dbg_xnb0", [128, C], BF16, kind="ExternalOutput")
        dbg_t["xnb1"] = nc.dram_tensor("dbg_xnb1", [128, C], BF16, kind="ExternalOutput")
        dbg_t["xnt0"] = nc.dram_tensor("dbg_xnt0", [128, C], BF16, kind="ExternalOutput")
        dbg_t["qkr0"] = nc.dram_tensor("dbg_qkr0", [128, C], BF16, kind="ExternalOutput")
        dbg_t["mv"] = nc.dram_tensor("dbg_mv", [128, 2], F32, kind="ExternalOutput")
        dbg_t["rs"] = nc.dram_tensor("dbg_rs", [128, 1], F32, kind="ExternalOutput")
        dbg_t["sums"] = nc.dram_tensor("dbg_sums", [128, 1], F32, kind="ExternalOutput")
        dbg_t["sumsq"] = nc.dram_tensor("dbg_sumsq", [128, 1], F32, kind="ExternalOutput")
        dbg_t["veps"] = nc.dram_tensor("dbg_veps", [128, 1], F32, kind="ExternalOutput")
        dbg_t["sqx"] = nc.dram_tensor("dbg_sqx", [128, C], F32, kind="ExternalOutput")
    if "qkv" in dbg:
        dbg_t["qt"] = nc.dram_tensor("dbg_qt", [4, 128, T], BF16, kind="ExternalOutput")
        dbg_t["kt"] = nc.dram_tensor("dbg_kt", [4, 128, T], BF16, kind="ExternalOutput")
        dbg_t["vaug"] = nc.dram_tensor("dbg_vaug", [128, NT, 8, 65], BF16, kind="ExternalOutput")
        dbg_t["negc"] = nc.dram_tensor("dbg_negc", [128, 8], F32, kind="ExternalOutput")
    if "attn" in dbg:
        dbg_t["ccin"] = nc.dram_tensor("dbg_ccin", [2, 4, 128, OWN], BF16, kind="ExternalOutput")
    if "oproj" in dbg:
        dbg_t["atny"] = nc.dram_tensor("dbg_atny", [OWN, C], F32, kind="ExternalOutput")
    if "ffn" in dbg:
        dbg_t["w1q"] = nc.dram_tensor("dbg_w1q", [128, 8, DH], F32, kind="ExternalOutput")
        dbg_t["w2q"] = nc.dram_tensor("dbg_w2q", [128, 32, C], F32, kind="ExternalOutput")
        dbg_t["h1"] = nc.dram_tensor("dbg_h1", [128, 32, FCH], F32, kind="ExternalOutput")
        dbg_t["hq0"] = nc.dram_tensor("dbg_hq0", [128, 32, FCH], BF16, kind="ExternalOutput")

    g = {}
    def sb(name, shape, dt):
        g[name] = st.enter_context(nc.sbuf_tensor(name, shape, dt))
        return g[name]

    sb("idb", [128, 128], BF16)
    sb("idf", [128, 128], F32)
    sb("mask", [128, 128], BF16)
    sb("cwb", [128, 128], F32)
    sb("bo_bc_s", [128, C], F32)
    sb("bqkv_s", [1, 1536], BF16)
    sb("ones_bf", [128, 128], BF16)
    sb("ones_f", [128, 128], F32)
    sb("nq2", [128, 8], F32)
    sb("nk2", [128, 8], F32)
    sb("negc", [128, 8], F32)
    sb("mw1", [128, 1], F32)
    sb("mw2", [128, 1], F32)
    sb("sw1", [128, 1], F32)
    sb("sw2", [128, 1], F32)
    sb("smallA", [128, 8], F32)
    sb("magic", [128, 1], F32)
    sb("barv", [1, 1], F32)
    sb("bara", [1, 1], F32)

    sm = {n: st.enter_context(nc.semaphore(n)) for n in SEMNAMES}
    cnt = {n: 0 for n in SEMNAMES}

    def inc(instr, name, k=1):
        instr.then_inc(sm[name], k)
        cnt[name] += k
        return instr

    class Rec:
        def __init__(self, e):
            self._e = e
            self.last = None
        def __getattr__(self, n):
            f = getattr(self._e, n)
            if not callable(f):
                return f
            def w(*a, **k):
                r = f(*a, **k)
                if n not in ("wait_ge", "nop", "dma_start") and hasattr(r, "then_inc"):
                    self.last = r
                return r
            return w

    def incl(rec, name, k=1, drain=False):
        e = rec._e
        if hasattr(e, "activation"):
            rec.last.then_inc(sm[name], k)
        elif hasattr(e, "tensor_tensor") and not hasattr(e, "matmul"):
            e.memset(g["barv"][:], 0.0).then_inc(sm[name], k)
        else:
            rec.last.then_inc(sm[name], k)
        cnt[name] += k

    stB = ExitStack()
    for nm, shape, dt in (("QT", [128, 4, T], BF16), ("KT", [128, 4, T], BF16),
                          ("VA", [128, NT, 8, 65], BF16)):
        g[nm] = stB.enter_context(nc.sbuf_tensor(nm, shape, dt))

    ST = STAGE
    blk = st.enter_context(nc.Block())
    PHW = [0]

    def barrier_close(engines=("sync", "tensor", "vector", "scalar", "gpsimd")):
        for e in engines:
            def mk(e):
                def f(eng):
                    if e == "vector":
                        inc(eng.memset(g["barv"][:], 0.0), "ph")
                    elif e == "scalar":
                        inc(eng.memzero(g["bara"][:]), "ph")
                    else:
                        inc(eng.nop(), "ph")
                return f
            getattr(blk, e)(mk(e))
        PHW[0] += len(engines)

    def bwait(eng):
        if PHW[0]:
            eng.wait_ge(sm["ph"], PHW[0])

    def ph01():
        p1 = ExitStack()
        def p1sb(name, shape, dt):
            return p1.enter_context(nc.sbuf_tensor(name, shape, dt))
        ropeC = p1sb("ropeCt", [128, NT, D], F32)
        ropeS = p1sb("ropeSt", [128, NT, D], F32)
        wqkv = p1sb("wqkv", [128, 8, 1536], BF16)
        xb = [p1sb(f"xb{i}", [128, C], F32) for i in range(2)]
        xnb = [p1sb(f"xnb{i}", [128, C], BF16) for i in range(2)]
        xnT = [p1sb(f"xnT{i}", [128, C], BF16) for i in range(2)]
        qkr = [p1sb(f"qkr{i}", [128, C], BF16) for i in range(2)]
        t2 = p1sb("t2", [128, C], F32)
        t3 = p1sb("t3", [128, C], F32)
        sqx = p1sb("sqx", [128, C], F32)
        sums = p1sb("sums", [128, 1], F32)
        sumsq = p1sb("sumsq", [128, 1], F32)
        mv = p1sb("mv", [128, 2], F32)
        veps = p1sb("veps", [128, 1], F32)
        lnv = p1sb("lnv", [128, 1], F32)
        rs = p1sb("rs", [128, 1], F32)
        nmrs = p1sb("nmrs", [128, 1], F32)
        dg = p1sb("dg", [128, 8], F32)
        sqb = [p1sb(f"sqb{i}", [128, C], F32) for i in range(2)]
        scr128 = p1sb("scr128", [128, 128], F32)
        prod8 = p1sb("prod8", [8, 1], F32)
        mq2c = p1sb("mq2c", [8, 1], F32)
        negc_col = p1sb("negc_col", [8, 1], F32)
        negc_row = p1sb("negc_row", [1, 8], F32)
        sumsB1 = p1sb("sumsB1", [128, NT], F32)
        sumsqB1 = p1sb("sumsqB1", [128, NT], F32)
        mvB1 = p1sb("mvB1", [128, NT], F32)
        vepsB1 = p1sb("vepsB1", [128, NT], F32)
        lnvB1 = p1sb("lnvB1", [128, NT], F32)
        rsB1 = p1sb("rsB1", [128, NT], F32)
        nmrsB1 = p1sb("nmrsB1", [128, NT], F32)

        pp = ExitStack()
        ps_qk = [pp.enter_context(nc.psum_tensor(f"ps_qk{i}", [128, 1024], F32)) for i in range(2)]
        ps_v = pp.enter_context(nc.psum_tensor("ps_v", [128, 512], F32))
        ps_trx = pp.enter_context(nc.psum_tensor("ps_trx", [128, 1024], F32))
        ps_trn = pp.enter_context(nc.psum_tensor("ps_trn", [128, 128], F32))
        ps_trA = ps_trx[:, 0:512].rearrange("p q -> p q") if False else None
        ps_trB = None

        @blk.sync
        def _(sy):
            sy = Rec(sy)
            sy.dma_start(out=g["idb"][:], in_=d_idb[:]).then_inc(sm["dc"], 16)
            sy.dma_start(out=g["idf"][:], in_=d_idf[:]).then_inc(sm["dc"], 16)
            sy.dma_start(out=g["mask"][:], in_=d_mask[:]).then_inc(sm["dc"], 16)
            sy.dma_start(out=g["cwb"][:], in_=d_cwb[:]).then_inc(sm["dc"], 16)
            sy.dma_start(out=g["bo_bc_s"][:], in_=d_bo[:]).then_inc(sm["dc"], 16)
            sy.dma_start(out=ropeC[:], in_=d_ropeC.ap().rearrange("p (t d) -> p t d", d=D)).then_inc(sm["dc"], 16)
            sy.dma_start(out=ropeS[:], in_=d_ropeS.ap().rearrange("p (t d) -> p t d", d=D)).then_inc(sm["dc"], 16)
            cnt["dc"] += 7 * 16
            for t in range(0, NT, 2):
                if t >= 2:
                    sy.wait_ge(sm["p3"], t - 1)
                sy.dma_start(out=xb[t % 2][:], in_=d_x[t * 128:(t + 1) * 128, :]).then_inc(sm["p2"], 16)
                cnt["p2"] += 16
            for t in range(NT):
                if t < 2:
                    sy.wait_ge(sm["p3"], NT)
                else:
                    sy.wait_ge(sm["a4"], t - 1)
                sy.dma_start(out=xb[t % 2][:], in_=d_x[t * 128:(t + 1) * 128, :]).then_inc(sm["dx"], 16)
                cnt["dx"] += 16
            sy.wait_ge(sm["dx"], cnt["dx"])

        @blk.gpsimd
        def _(gp):
            gp = Rec(gp)
            gp.dma_start(out=g["bqkv_s"][:], in_=d_bqkv[:]).then_inc(sm["dg1"], 16)
            cnt["dg1"] += 16
            for i in range(8):
                gp.dma_start(out=wqkv[:, i, :], in_=d_wqkvT[i * 128:(i + 1) * 128, :]).then_inc(sm["dg1"], 16)
            cnt["dg1"] += 8 * 16
            for t in range(1, NT, 2):
                gp.wait_ge(sm["p3"], t - 1)
                gp.dma_start(out=xb[t % 2][:], in_=d_x[t * 128:(t + 1) * 128, :]).then_inc(sm["dg1"], 16)
                cnt["dg1"] += 16
            gp.wait_ge(sm["dg1"], cnt["dg1"])


        dc_all = cnt["dc"]
        dg1_all = cnt["dg1"]

        @blk.vector
        def _(v):
            v = Rec(v)
            v.memset(g["ones_bf"][:], 1.0)
            v.memset(g["ones_f"][:], 1.0)
            v.memset(g["magic"][:], MAGIC)
            v.memset(g["nq2"][:], 0.0)
            v.memset(g["nk2"][:], 0.0)
            v.memset(g["VA"][:, :, :, 64:65], 1.0)
            incl(v, "b1")
            sqpp = [sqx, t2]
            for t in range(NT):
                if t % 2 == 0:
                    v.wait_ge(sm["p2"], 16 * (t // 2 + 1))
                else:
                    v.wait_ge(sm["dg1"], 144 + 16 * ((t + 1) // 2))
                v.reduce_sum(sumsB1[:, t:t + 1], xb[t % 2][:], AX.X)
                v.wait_ge(sm["p1"], t + 1)
                v.reduce_sum(sumsqB1[:, t:t + 1], sqpp[t % 2][:], AX.X)
                incl(v, "p3", drain=True)
            v.tensor_scalar(mvB1[:], sumsB1[:], 1.0 / C, None, ALU.mult)
            v.drain()
            v.tensor_tensor(vepsB1[:], mvB1[:], mvB1[:], ALU.mult)
            v.drain()
            v.scalar_tensor_tensor(vepsB1[:], sumsqB1[:], 1.0 / C, vepsB1[:],
                                   ALU.mult, ALU.subtract)
            v.drain()
            v.tensor_scalar(vepsB1[:], vepsB1[:], EPS, None, ALU.add)
            incl(v, "a1", drain=True)
            v.wait_ge(sm["p1"], NT + 1)
            v.scalar_tensor_tensor(nmrsB1[:], mvB1[:], -1.0, rsB1[:], ALU.mult, ALU.mult)
            incl(v, "a3", drain=True)
            for t in range(NT):
                v.wait_ge(sm["a5"], t + 1)
                if ST >= 7:
                    v.tensor_copy(xnT[t % 2][:, 0:512], ps_trx[:, 0:512])
                    v.tensor_copy(xnT[t % 2][:, 512:1024], ps_trx[:, 512:1024])
                incl(v, "a6", drain=True)
                v.wait_ge(sm["a7"], t + 1)
                if t >= 2:
                    v.wait_ge(sm["b2"], t - 1)
                    v.wait_ge(sm["c3"], t - 1)
                crep = ropeC[:, t:t + 1, :].broadcast_to([128, 16, D])
                srep = ropeS[:, t:t + 1, :].broadcast_to([128, 16, D])
                qk = ps_qk[t % 2][:].rearrange("p (r d) -> p r d", d=D)
                v.tensor_tensor(t2[:].rearrange("p (r d) -> p r d", d=D), qk, crep, ALU.mult)
                qk_sw = bass.AP(ps_qk[t % 2][:].tensor, 1, [[1024, 128], [64, 16], [2, 32], [-1, 2]])
                srep4 = ropeS[:, t, :].rearrange("p (a b) -> p a b", b=2).unsqueeze(1).broadcast_to([128, 16, 32, 2])
                v.tensor_tensor(t3[:].rearrange("p (r a b) -> p r a b", a=32, b=2), qk_sw,
                                srep4, ALU.mult)
                v.tensor_add(qkr[t % 2][:], t2[:], t3[:])
                v.tensor_copy(g["VA"][:, t, :, 0:64],
                              ps_v[:].rearrange("p (h d) -> p h d", d=64))
                incl(v, "a8")
                v.wait_ge(sm["b2"], t + 1)
                if ST >= 12:
                    for b_ in range(4):
                        v.tensor_copy(g["QT"][:, b_, t * 128:(t + 1) * 128],
                                      ps_trx[:, b_ * 128:(b_ + 1) * 128])
                    for b_ in range(4):
                        v.tensor_copy(g["KT"][:, b_, t * 128:(t + 1) * 128],
                                      ps_trx[:, 512 + b_ * 128:512 + (b_ + 1) * 128])
                incl(v, "b3", drain=True)
                v.wait_ge(sm["c3"], t + 1)
                if ST >= 13:
                    sq_ = sqb[t % 2][:]
                    v.tensor_reduce(dg[:].unsqueeze(2), sq_[:, 0:512].rearrange("p (h d) -> p h d", d=64),
                                    AX.X, ALU.add)
                    v.drain()
                    v.tensor_tensor(g["nq2"][:], g["nq2"][:], dg[:], ALU.max)
                    v.tensor_reduce(dg[:].unsqueeze(2), sq_[:, 512:1024].rearrange("p (h d) -> p h d", d=64),
                                    AX.X, ALU.add)
                    v.drain()
                    v.tensor_tensor(g["nk2"][:], g["nk2"][:], dg[:], ALU.max)
                incl(v, "c4", drain=True)
            if ST >= 14:
                v.memset(scr128[:], 0.0)
                v.drain()
                v.tensor_copy(scr128[:, 0:8], g["nq2"][:])
            incl(v, "b6", drain=True)
            v.wait_ge(sm["c1"], 1)
            if ST >= 14:
                v.tensor_reduce(mq2c[:], ps_trn[0:8, :], AX.X, ALU.max)
                v.memset(scr128[:], 0.0)
                v.drain()
                v.tensor_copy(scr128[:, 0:8], g["nk2"][:])
            incl(v, "b6", drain=True)
            v.wait_ge(sm["c1"], 2)
            if ST >= 14:
                v.tensor_reduce(prod8[:], ps_trn[0:8, :], AX.X, ALU.max)
                v.drain()
                v.tensor_tensor(prod8[:], prod8[:], mq2c[:], ALU.mult)
            incl(v, "c4", drain=True)
            v.wait_ge(sm["b7"], 1)
            if ST >= 14:
                v.tensor_scalar(negc_col[:], negc_col[:], -1.02 / 8.0, 45.0, ALU.mult, ALU.add)
                v.memset(scr128[:], 0.0)
                v.drain()
                v.tensor_copy(scr128[0:8, 0:1], negc_col[:])
            incl(v, "c2", drain=True)
            v.wait_ge(sm["c1"], 3)
            if ST >= 14:
                v.tensor_copy(negc_row[:], ps_trn[0:1, 0:8])
            incl(v, "c2", drain=True)
            v.wait_ge(sm["c1"], 4)
            if ST >= 14:
                v.tensor_copy(g["negc"][:], ps_trn[:, 0:8])
            incl(v, "b8", drain=True)

        @blk.scalar
        def _(a):
            a = Rec(a)
            sqpp = [sqx, t2]
            for t in range(NT):
                if t % 2 == 0:
                    a.wait_ge(sm["p2"], 16 * (t // 2 + 1))
                else:
                    a.wait_ge(sm["dg1"], 144 + 16 * ((t + 1) // 2))
                if t >= 2:
                    a.wait_ge(sm["p3"], t - 1)
                a.activation(sqpp[t % 2][:], xb[t % 2][:], AF.Square)
                incl(a, "p1", drain=True)
            a.wait_ge(sm["a1"], 1)
            a.activation(lnvB1[:], vepsB1[:], AF.Ln)
            a.drain()
            a.activation(rsB1[:], lnvB1[:], AF.Exp, scale=-0.5)
            incl(a, "p1", drain=True)
            for t in range(NT):
                a.wait_ge(sm["dx"], 16 * min(t + 2, NT))
                a.wait_ge(sm["a3"], 1)
                if t >= 2:
                    a.wait_ge(sm["a5"], t - 1)
                if ST >= 5:
                    a.activation(xnb[t % 2][:], xb[t % 2][:], AF.Identity,
                                 bias=nmrsB1[:, t:t + 1], scale=rsB1[:, t:t + 1])
                incl(a, "a4", drain=True)
                a.wait_ge(sm["a8"], t + 1)
                if t >= 2:
                    a.wait_ge(sm["c4"], t - 1)
                if ST >= 10:
                    a.activation(sqb[t % 2][:], qkr[t % 2][:], AF.Square)
                incl(a, "c3", drain=True)
            a.wait_ge(sm["c4"], NT + 1)
            if ST >= 14:
                a.activation(negc_col[:], prod8[:], AF.Sqrt)
            incl(a, "b7", drain=True)

        @blk.tensor
        def _(te):
            te = Rec(te)
            te.wait_ge(sm["dc"], dc_all)
            te.wait_ge(sm["dg1"], dg1_all)
            te.wait_ge(sm["b1"], 1)
            for t in range(NT):
                te.wait_ge(sm["a4"], t + 1)
                if t >= 1:
                    te.wait_ge(sm["b3"], t)
                if ST >= 6:
                    for b_ in range(8):
                        te.matmul(ps_trx[:, b_ * 128:(b_ + 1) * 128],
                                  xnb[t % 2][:, b_ * 128:(b_ + 1) * 128], g["idb"][:],
                                  start=True, stop=True)
                incl(te, "a5")
                te.wait_ge(sm["a6"], t + 1)
                if t >= 2:
                    te.wait_ge(sm["a8"], t - 1)
                if ST >= 8:
                    for dk in range(8):
                        for n in range(2):
                            te.matmul(ps_qk[t % 2][:, n * 512:(n + 1) * 512],
                                      xnT[t % 2][:, dk * 128:(dk + 1) * 128],
                                      wqkv[:, dk, n * 512:(n + 1) * 512],
                                      start=(dk == 0), stop=False)
                    for n in range(2):
                        te.matmul(ps_qk[t % 2][:, n * 512:(n + 1) * 512],
                                  g["ones_bf"][0:1, :],
                                  g["bqkv_s"][0:1, n * 512:(n + 1) * 512],
                                  start=False, stop=True)
                    if t >= 1:
                        te.wait_ge(sm["a8"], t)
                    for dk in range(8):
                        te.matmul(ps_v[:],
                                  xnT[t % 2][:, dk * 128:(dk + 1) * 128],
                                  wqkv[:, dk, 1024:1536],
                                  start=(dk == 0), stop=False)
                    te.matmul(ps_v[:],
                              g["ones_bf"][0:1, :],
                              g["bqkv_s"][0:1, 1024:1536],
                              start=False, stop=True)
                incl(te, "a7")
                te.wait_ge(sm["a8"], t + 1)
                te.wait_ge(sm["a6"], t + 1)
                if ST >= 11:
                    for b_ in range(8):
                        te.matmul(ps_trx[:, b_ * 128:(b_ + 1) * 128],
                                  qkr[t % 2][:, b_ * 128:(b_ + 1) * 128], g["idb"][:],
                                  start=True, stop=True)
                incl(te, "b2")
            te.wait_ge(sm["b6"], 1)
            if ST >= 14:
                te.matmul(ps_trn[:], scr128[:], g["idf"][:], start=True, stop=True)
            incl(te, "c1")
            te.wait_ge(sm["b6"], 2)
            if ST >= 14:
                te.matmul(ps_trn[:], scr128[:], g["idf"][:], start=True, stop=True)
            incl(te, "c1")
            te.wait_ge(sm["c2"], 1)
            if ST >= 14:
                te.matmul(ps_trn[:], scr128[:], g["idf"][:], start=True, stop=True)
            incl(te, "c1")
            te.wait_ge(sm["c2"], 2)
            if ST >= 14:
                te.matmul(ps_trn[:, 0:8], g["ones_f"][0:1, 0:128], negc_row[:],
                          start=True, stop=True)
            incl(te, "c1")

        if "ln" in dbg:
            @blk.gpsimd
            def _(gp):
                gp.wait_ge(sm["b8"], 1)
                for nm, src in (("xnb0", xnb[0][:]), ("xnb1", xnb[1][:]), ("xnt0", xnT[0][:]),
                                ("qkr0", qkr[0][:]), ("mv", mv[:]), ("rs", rs[:]),
                                ("sums", sums[:]), ("sumsq", sumsq[:]), ("veps", veps[:]),
                                ("sqx", sqx[:])):
                    gp.dma_start(out=dbg_t[nm].ap(), in_=src).then_inc(sm["dbg"], 16)
                    cnt["dbg"] += 16
                gp.wait_ge(sm["dbg"], cnt["dbg"])

        if "qkv" in dbg:
            @blk.gpsimd
            def _(gp):
                gp = Rec(gp)
                gp.wait_ge(sm["b8"], 1)
                gp.wait_ge(sm["a8"], NT)
                for b_ in range(4):
                    gp.dma_start(out=dbg_t["qt"][b_], in_=g["QT"][:, b_, :]).then_inc(sm["dbg"], 16)
                    gp.dma_start(out=dbg_t["kt"][b_], in_=g["KT"][:, b_, :]).then_inc(sm["dbg"], 16)
                gp.dma_start(out=dbg_t["vaug"][:], in_=g["VA"][:]).then_inc(sm["dbg"], 16)
                gp.dma_start(out=dbg_t["negc"][:], in_=g["negc"][:]).then_inc(sm["dbg"], 16)
                cnt["dbg"] += 160
                gp.wait_ge(sm["dbg"], cnt["dbg"])

        barrier_close()
        pp.close()
        p1.close()

    ph01()
    if phases < 2:
        stB.close()
        st.close()
        return nc
    g["w1qT"] = st.enter_context(nc.sbuf_tensor("w1qT", [128, 8, DH], FP8, side="right"))

    def ph2():
        bw = PHW[0]
        p2 = ExitStack()
        def p2sb(name, shape, dt):
            return p2.enter_context(nc.sbuf_tensor(name, shape, dt))
        PT = [[p2sb(f"PT{a}{i}", [128, QCH], BF16) for i in range(4)] for a in range(2)]
        uaT = [[p2sb(f"uaT{a}{i}", [65, QCH], F32) for i in range(2)] for a in range(2)]
        rcp = [[p2sb(f"rcp{a}{i}", [65, QCH], F32) for i in range(2)] for a in range(2)]
        stg = [p2sb(f"stg{a}", [64, QCH], BF16) for a in range(2)]
        wbuf = [p2sb(f"wbuf{i}", [128, 2048], F32) for i in range(2)]
        utmp = p2sb("utmp", [128, 2048], F32)
        wsum = p2sb("wsum", [128, 1], F32)
        wpart = p2sb("wpart", [128, 1], F32)
        tsum = p2sb("tsum", [1, 1], F32)

        pp = ExitStack()
        ps_S = [[pp.enter_context(nc.psum_tensor(f"ps_S{a}{i}", [128, QCH], F32))
                 for i in range(2)] for a in range(2)]
        ps_oA = pp.enter_context(nc.psum_tensor("ps_oA", [65, QCH], F32))
        ps_oB = pp.enter_context(nc.psum_tensor("ps_oB", [65, QCH], F32))
        ps_bc = pp.enter_context(nc.psum_tensor("ps_bc", [64, QCH], F32))
        ps_ws = pp.enter_context(nc.psum_tensor("ps_ws", [128, 2], F32))

        iters = []
        for qc in range(NQC):
            for pj in range(4):
                iters.append((pj, qc, list(range((qc + 1) * (QCH // 128)))))
        kb_list = []
        prevpar = []
        prevpt = []
        lastpar = {0: None, 1: None}
        lastpt = {i: None for i in range(4)}
        diag_of = {}
        for it, (pj, qc, kbs) in enumerate(iters):
            for kb in kbs:
                qlo = max(qc * QCH, kb * 128)
                w = qc * QCH + QCH - qlo
                diag = kb * 128 >= qc * QCH
                ki = len(kb_list)
                if diag:
                    diag_of[ki] = len(diag_of)
                kb_list.append((it, kb, qlo, w, diag))
                prevpar.append(lastpar[kb & 1])
                lastpar[kb & 1] = ki
                prevpt.append(lastpt[kb & 3])
                lastpt[kb & 3] = ki
        ki_of = {}
        k0 = 0
        for it, (pj, qc, kbs) in enumerate(iters):
            ki_of[it] = k0
            k0 += len(kbs)

        bs = dict(cnt)
        def w1_chunk(i):
            return i // 2, (i % 2) * 2048

        @blk.sync
        def _(sy):
            sy = Rec(sy)
            bwait(sy)
            for i in range(16):
                dk, co = w1_chunk(i)
                if i >= 2:
                    sy.wait_ge(sm["b2"], bs["b2"] + i - 1)
                sy.dma_start(out=wbuf[i % 2][:],
                             in_=d_w1T[dk * 128:(dk + 1) * 128, co:co + 2048]).then_inc(sm["dw"], 16)
                cnt["dw"] += 16
            for i in range(16):
                dk, co = w1_chunk(i)
                if i < 2:
                    sy.wait_ge(sm["b2"], bs["b2"] + 16)
                else:
                    sy.wait_ge(sm["b7"], bs["b7"] + i - 1)
                sy.dma_start(out=wbuf[i % 2][:],
                             in_=d_w1T[dk * 128:(dk + 1) * 128, co:co + 2048]).then_inc(sm["dw"], 16)
                cnt["dw"] += 16
            sy.wait_ge(sm["dw"], cnt["dw"])

        @blk.tensor
        def _(te):
            te = Rec(te)
            bwait(te)
            def emit_S(ki):
                it, kb, qlo, w, diag = kb_list[ki]
                pj = iters[it][0]
                pv = prevpar[ki]
                if pv is not None:
                    te.wait_ge(sm["a2"], bs["a2"] + pv + 1)
                te.matmul(ps_S[0][kb % 2][:, 0:w],
                          g["KT"][0:64, pj, kb * 128:(kb + 1) * 128],
                          g["QT"][0:64, pj, qlo:qlo + w],
                          start=True, stop=True, tile_position=(0, 0))
                te.matmul(ps_S[1][kb % 2][:, 0:w],
                          g["KT"][64:128, pj, kb * 128:(kb + 1) * 128],
                          g["QT"][64:128, pj, qlo:qlo + w],
                          start=True, stop=True, tile_position=(64, 0))
                incl(te, "a1")

            for it, (pj, qc, kbs) in enumerate(iters):
                kib = ki_of[it]
                for j, kb in enumerate(kbs):
                    ki = kib + j
                    _, _, qlo, w, diag = kb_list[ki]
                    if j == 0:
                        emit_S(ki)
                        if len(kbs) > 1:
                            emit_S(ki + 1)
                    te.wait_ge(sm["a2"], bs["a2"] + ki + 1)
                    if diag:
                        te.wait_ge(sm["a3"], bs["a3"] + diag_of[ki] + 1)
                    if kb == 0 and it >= 1:
                        te.wait_ge(sm["a5"], bs["a5"] + 2 * it)
                    dst = qlo - qc * QCH
                    last = (kb == kbs[-1])
                    te.matmul(ps_oA[:, dst:dst + w], g["VA"][:, kb, 2 * pj, :],
                              PT[0][kb % 4][:, 0:w], start=(kb == 0), stop=last)
                    incl(te, "a4")
                    te.matmul(ps_oB[:, dst:dst + w], g["VA"][:, kb, 2 * pj + 1, :],
                              PT[1][kb % 4][:, 0:w], start=(kb == 0), stop=last)
                    incl(te, "a4")
                    if j + 2 < len(kbs):
                        emit_S(kib + j + 2)
                if it >= 1:
                    itp = it - 1
                    for a_ in range(2):
                        m = 2 * itp + a_
                        te.wait_ge(sm["a6"], bs["a6"] + m + 1)
                        if m >= 1:
                            te.wait_ge(sm["a8"], bs["a8"] + m)
                        te.matmul(ps_bc[:], g["ones_f"][64:65, 0:64], rcp[a_][itp % 2][64:65, :],
                                  start=True, stop=True, tile_position=(64, 0))
                        incl(te, "a7")
            itp = len(iters) - 1
            for a_ in range(2):
                m = 2 * itp + a_
                te.wait_ge(sm["a6"], bs["a6"] + m + 1)
                te.wait_ge(sm["a8"], bs["a8"] + m)
                te.matmul(ps_bc[:], g["ones_f"][64:65, 0:64], rcp[a_][itp % 2][64:65, :],
                          start=True, stop=True, tile_position=(64, 0))
                incl(te, "a7")
            te.wait_ge(sm["b5"], bs["b5"] + 1)
            te.matmul(ps_ws[0:1, 0:1], wsum[:], g["ones_f"][:, 0:1], start=True, stop=True)
            incl(te, "c3")

        @blk.tensor
        def _(te):
            te = Rec(te)
            te.wait_ge(sm["c4"], bs["c4"] + 1)
            te.matmul(ps_ws[:, 1:2], g["ones_f"][0:1, 0:128], tsum[:], start=True, stop=True)
            incl(te, "c5")

        @blk.scalar
        def _(a):
            a = Rec(a)
            bwait(a)
            for it, (pj, qc, kbs) in enumerate(iters):
                kib = ki_of[it]
                for j, kb in enumerate(kbs):
                    ki = kib + j
                    a.wait_ge(sm["a1"], bs["a1"] + ki + 1)
                    pv = prevpt[ki]
                    if pv is not None:
                        a.wait_ge(sm["a4"], bs["a4"] + 2 * (pv + 1))
                    _, _, qlo, w, diag = kb_list[ki]
                    a.activation(PT[0][kb % 4][:, 0:w], ps_S[0][kb % 2][:, 0:w],
                                 AF.Exp, bias=g["negc"][:, 2 * pj:2 * pj + 1], scale=0.125)
                    a.activation(PT[1][kb % 4][:, 0:w], ps_S[1][kb % 2][:, 0:w],
                                 AF.Exp, bias=g["negc"][:, 2 * pj + 1:2 * pj + 2], scale=0.125)
                    incl(a, "a2", drain=True)

        @blk.vector
        def _(v):
            v = Rec(v)
            bwait(v)
            for it, (pj, qc, kbs) in enumerate(iters):
                kib = ki_of[it]
                for j, kb in enumerate(kbs):
                    ki = kib + j
                    _, _, qlo, w, diag = kb_list[ki]
                    if diag:
                        v.wait_ge(sm["a2"], bs["a2"] + ki + 1)
                        v.tensor_tensor(PT[0][kb % 4][:, 0:128], PT[0][kb % 4][:, 0:128],
                                        g["mask"][:], ALU.mult)
                        v.tensor_tensor(PT[1][kb % 4][:, 0:128], PT[1][kb % 4][:, 0:128],
                                        g["mask"][:], ALU.mult)
                        incl(v, "a3", drain=True)
                ki_last = kib + len(kbs) - 1
                v.wait_ge(sm["a4"], bs["a4"] + 2 * (ki_last + 1))
                for a_, ps_o in ((0, ps_oA), (1, ps_oB)):
                    ua = uaT[a_][it % 2]
                    v.tensor_copy(ua[:], ps_o[:])
                    v.tensor_scalar(ua[64:65, :], ua[64:65, :], SUM_FLOOR, None, ALU.max)
                    incl(v, "a5", drain=True)
                    if it >= 2:
                        v.wait_ge(sm["a7"], bs["a7"] + 2 * (it - 2) + a_ + 1)
                    v.reciprocal(rcp[a_][it % 2][64:65, :], ua[64:65, :])
                    incl(v, "a6", drain=True)
                if it >= 1:
                    itp = it - 1
                    for a_ in range(2):
                        m = 2 * itp + a_
                        v.wait_ge(sm["a7"], bs["a7"] + m + 1)
                        if m >= 1:
                            v.wait_ge(sm["dcc"], bs["dcc"] + 16 * m)
                        v.tensor_tensor(stg[a_][:], uaT[a_][itp % 2][0:64, :], ps_bc[:], ALU.mult)
                        incl(v, "a8", drain=True)
                if it < 16:
                    v.wait_ge(sm["dw"], bs["dw"] + 16 * min(it + 2, 16))
                    v.tensor_reduce(wpart[:], wbuf[it % 2][:], AX.X, ALU.add,
                                    apply_absolute_value=True)
                    v.drain()
                    if it == 0:
                        v.tensor_copy(wsum[:], wpart[:])
                    else:
                        v.tensor_add(wsum[:], wsum[:], wpart[:])
                    incl(v, "b2", drain=True)
                    if it == 15:
                        incl(v, "b5")
            itp = len(iters) - 1
            for a_ in range(2):
                m = 2 * itp + a_
                v.wait_ge(sm["a7"], bs["a7"] + m + 1)
                v.wait_ge(sm["dcc"], bs["dcc"] + 16 * m)
                v.tensor_tensor(stg[a_][:], uaT[a_][itp % 2][0:64, :], ps_bc[:], ALU.mult)
                incl(v, "a8", drain=True)

        @blk.vector
        def _(v):
            v = Rec(v)
            v.wait_ge(sm["c3"], bs["c3"] + 1)
            v.tensor_copy(tsum[:], ps_ws[0:1, 0:1])
            incl(v, "c4", drain=True)
            v.wait_ge(sm["c5"], bs["c5"] + 1)
            v.tensor_scalar(g["mw1"][:], ps_ws[:, 1:2], 1.0 / (C * DH), CLIP, ALU.mult, ALU.max)
            v.drain()
            v.reciprocal(g["sw1"][:], g["mw1"][:])
            incl(v, "b6", drain=True)
            for ii in range(16):
                dk, co = w1_chunk(ii)
                v.wait_ge(sm["b7"], bs["b7"] + ii + 1)
                v.tensor_scalar(utmp[:], utmp[:], MAGIC, 1.0, ALU.subtract, ALU.min)
                v.tensor_scalar(g["w1qT"][:, dk, co:co + 2048], utmp[:], -1.0, None, ALU.max)
                incl(v, "b3", drain=True)

        @blk.scalar
        def _(a):
            a = Rec(a)
            for ii in range(16):
                a.wait_ge(sm["b6"], bs["b6"] + 1)
                a.wait_ge(sm["dw"], bs["dw"] + 16 * min(16 + ii + 2, 32))
                if ii >= 1:
                    a.wait_ge(sm["b3"], bs["b3"] + ii)
                a.activation(utmp[:], wbuf[ii % 2][:], AF.Identity, bias=g["magic"][:], scale=g["sw1"][:])
                incl(a, "b7", drain=True)

        @blk.gpsimd
        def _(gp):
            gp = Rec(gp)
            bwait(gp)
            for it in range(len(iters)):
                pj, qc, kbs = iters[it]
                half, qs = qc // (NQC // 2), (qc % (NQC // 2)) * QCH
                for a_ in range(2):
                    m = 2 * it + a_
                    gp.wait_ge(sm["a8"], bs["a8"] + m + 1)
                    gp.dma_start(out=d_ccin[half, pj, 64 * a_:64 * a_ + 64, qs:qs + QCH],
                                 in_=stg[a_][:]).then_inc(sm["dcc"], 16)
                    cnt["dcc"] += 16
                if it == len(iters) // 2 - 1:
                    gp.wait_ge(sm["dcc"], bs["dcc"] + 16 * (2 * it + 2))
                    gp.collective_compute(
                        "AllGather", ALU.bypass,
                        replica_groups=[[0, 1], [2, 3], [4, 5], [6, 7]],
                        ins=[d_ccin[0].opt()], outs=[d_ccoutA.ap().opt()],
                    ).then_inc(sm["c2"], 1)
                    cnt["c2"] += 1
            gp.wait_ge(sm["dcc"], cnt["dcc"])
            gp.collective_compute(
                "AllGather", ALU.bypass,
                replica_groups=[[0, 1], [2, 3], [4, 5], [6, 7]],
                ins=[d_ccin[1].opt()], outs=[d_ccoutB.ap().opt()],
            ).then_inc(sm["c2"], 1)
            cnt["c2"] += 1
            gp.wait_ge(sm["c2"], cnt["c2"])

        if "attn" in dbg:
            @blk.gpsimd
            def _(gp):
                gp = Rec(gp)
                gp.dma_start(out=dbg_t["ccin"].ap(), in_=d_ccin.ap()).then_inc(sm["dbg"], 16)
                cnt["dbg"] += 16
                gp.wait_ge(sm["dbg"], cnt["dbg"])

        barrier_close()
        pp.close()
        p2.close()

    ph2()
    stB.close()
    if phases < 3:
        st.close()
        return nc
    g["w2qT"] = st.enter_context(nc.sbuf_tensor("w2qT", [128, 32, C], FP8, side="right"))

    def ph26():
        p3 = ExitStack()
        def p3sb(name, shape, dt):
            return p3.enter_context(nc.sbuf_tensor(name, shape, dt))
        woT = p3sb("woT_s", [128, 16, C], BF16)
        gath = p3sb("gath", [128, 16, OWN], BF16)
        ob = [p3sb(f"ob{i}", [128, C], F32) for i in range(2)]
        wbuf = [p3sb(f"w2b{i}", [128, 2048], F32) for i in range(2)]
        utmp = p3sb("utmp2", [128, 2048], F32)
        wsum = p3sb("wsum2", [128, 1], F32)
        wpart = p3sb("wpart2", [128, 1], F32)
        tsum = p3sb("tsum2", [1, 1], F32)

        pp = ExitStack()
        ps_o = [pp.enter_context(nc.psum_tensor(f"ps_op{i}", [128, C], F32)) for i in range(2)]
        ps_ws = pp.enter_context(nc.psum_tensor("ps_ws2", [128, 2], F32))

        bs = dict(cnt)

        @blk.sync
        def _(sy):
            sy = Rec(sy)
            bwait(sy)
            for i in range(18):
                if 2 <= i < 16:
                    sy.wait_ge(sm["b2"], bs["b2"] + i - 1)
                if i < 16:
                    sy.dma_start(out=wbuf[i % 2][:].rearrange("p (a q) -> p a q", a=2),
                                 in_=d_w2T[i * 256:(i + 1) * 256, :].rearrange("(a p) q -> p a q", p=128)).then_inc(sm["dw"], 16)
                    cnt["dw"] += 16
                if i >= 2:
                    tt = i - 2
                    sy.wait_ge(sm["a3"], bs["a3"] + tt + 1)
                    sy.dma_start(out=d_atny[tt * 128:(tt + 1) * 128, :], in_=ob[tt % 2][:]).then_inc(sm["day"], 16)
                    cnt["day"] += 16
            for i in range(16):
                if i < 2:
                    sy.wait_ge(sm["b2"], bs["b2"] + 16)
                else:
                    sy.wait_ge(sm["b7"], bs["b7"] + i - 1)
                sy.dma_start(out=wbuf[i % 2][:].rearrange("p (a q) -> p a q", a=2),
                             in_=d_w2T[i * 256:(i + 1) * 256, :].rearrange("(a p) q -> p a q", p=128)).then_inc(sm["dw"], 16)
                cnt["dw"] += 16
            sy.wait_ge(sm["dw"], cnt["dw"])
            sy.wait_ge(sm["day"], cnt["day"])

        @blk.gpsimd
        def _(gp):
            gp = Rec(gp)
            bwait(gp)
            for kt in range(16):
                gp.dma_start(out=woT[:, kt, :], in_=d_woT[kt * 128:(kt + 1) * 128, :]).then_inc(sm["dg1"], 16)
                cnt["dg1"] += 16
            for kt in range(16):
                r_, h_, b_ = kt // 8, (kt // 4) % 2, kt % 4
                src_cc = d_ccoutA if h_ == 0 else d_ccoutB
                gp.dma_start(out=gath[:, kt, :],
                             in_=src_cc[r_, b_]).then_inc(sm["dcc"], 16)
                cnt["dcc"] += 16
            gp.wait_ge(sm["dcc"], cnt["dcc"])
            gp.wait_ge(sm["dg1"], cnt["dg1"])


        dc_woT = cnt["dg1"]

        @blk.tensor
        def _(te):
            te = Rec(te)
            bwait(te)
            te.wait_ge(sm["dg1"], dc_woT)
            te.wait_ge(sm["dcc"], bs["dcc"] + 16 * 16)
            for tt in range(16):
                if tt >= 2:
                    te.wait_ge(sm["a3"], bs["a3"] + tt - 1)
                for kt in range(16):
                    for n in range(2):
                        te.matmul(ps_o[tt % 2][:, n * 512:(n + 1) * 512],
                                  gath[:, kt, tt * 128:(tt + 1) * 128],
                                  woT[:, kt, n * 512:(n + 1) * 512],
                                  start=(kt == 0), stop=(kt == 15))
                incl(te, "a2")
            te.wait_ge(sm["b5"], bs["b5"] + 1)
            te.matmul(ps_ws[0:1, 0:1], wsum[:], g["ones_f"][:, 0:1], start=True, stop=True)
            incl(te, "c3")

        @blk.tensor
        def _(te):
            te = Rec(te)
            te.wait_ge(sm["c4"], bs["c4"] + 1)
            te.matmul(ps_ws[:, 1:2], g["ones_f"][0:1, 0:128], tsum[:], start=True, stop=True)
            incl(te, "c5")

        @blk.vector
        def _(v):
            v = Rec(v)
            bwait(v)
            for tt in range(16):
                v.wait_ge(sm["dw"], bs["dw"] + 16 * min(tt + 2, 16))
                v.tensor_reduce(wpart[:], wbuf[tt % 2][:], AX.X, ALU.add,
                                apply_absolute_value=True)
                v.drain()
                if tt == 0:
                    v.tensor_copy(wsum[:], wpart[:])
                else:
                    v.tensor_add(wsum[:], wsum[:], wpart[:])
                incl(v, "b2", drain=True)
                if tt == 15:
                    incl(v, "b5")
                v.wait_ge(sm["a2"], bs["a2"] + tt + 1)
                if tt >= 2:
                    v.wait_ge(sm["day"], bs["day"] + 16 * tt)
                v.tensor_tensor(ob[tt % 2][:, 0:512], ps_o[tt % 2][:, 0:512],
                                g["bo_bc_s"][:, 0:512], ALU.add)
                v.tensor_tensor(ob[tt % 2][:, 512:1024], ps_o[tt % 2][:, 512:1024],
                                g["bo_bc_s"][:, 512:1024], ALU.add)
                incl(v, "a3", drain=True)
            v.wait_ge(sm["c3"], bs["c3"] + 1)
            v.tensor_copy(tsum[:], ps_ws[0:1, 0:1])
            incl(v, "c4", drain=True)
            v.wait_ge(sm["c5"], bs["c5"] + 1)
            v.tensor_scalar(g["mw2"][:], ps_ws[:, 1:2], 1.0 / (C * DH), CLIP, ALU.mult, ALU.max)
            v.drain()
            v.reciprocal(g["sw2"][:], g["mw2"][:])
            incl(v, "b6", drain=True)
            for i in range(16):
                v.wait_ge(sm["b7"], bs["b7"] + i + 1)
                v.tensor_scalar(utmp[:], utmp[:], MAGIC, 1.0, ALU.subtract, ALU.min)
                for a_ in range(2):
                    v.tensor_scalar(g["w2qT"][:, 2 * i + a_, :], utmp[:, a_ * 1024:(a_ + 1) * 1024],
                                    -1.0, None, ALU.max)
                incl(v, "b3", drain=True)

        @blk.scalar
        def _(a):
            a = Rec(a)
            bwait(a)
            for i in range(16):
                a.wait_ge(sm["b6"], bs["b6"] + 1)
                a.wait_ge(sm["dw"], bs["dw"] + 16 * min(16 + i + 2, 32))
                if i >= 1:
                    a.wait_ge(sm["b3"], bs["b3"] + i)
                a.activation(utmp[:], wbuf[i % 2][:], AF.Identity, bias=g["magic"][:], scale=g["sw2"][:])
                incl(a, "b7", drain=True)

        if "oproj" in dbg:
            @blk.gpsimd
            def _(gp):
                gp = Rec(gp)
                gp.wait_ge(sm["day"], cnt["day"])
                gp.dma_start(out=dbg_t["atny"].ap(), in_=d_atny.ap()).then_inc(sm["dbg"], 16)
                cnt["dbg"] += 16
                gp.wait_ge(sm["dbg"], cnt["dbg"])

        barrier_close()
        pp.close()
        p3.close()

    ph26()
    if phases < 4:
        st.close()
        return nc

    def ph3():
        p4 = ExitStack()
        def p4sb(name, shape, dt):
            return p4.enter_context(nc.sbuf_tensor(name, shape, dt))
        xf = [p4sb(f"xf{i}", [128, C], F32) for i in range(2)]
        nx = [p4sb(f"nx{i}", [128, C], BF16) for i in range(2)]
        nxT = [p4sb(f"nxT{i}", [128, 8, FCH], BF16) for i in range(2)]
        hwin = [p4sb(f"hwin{i}", [128, 32, FCH], F32) for i in range(2)]
        cvo = [p4sb(f"cvo{i}", [128, 32, FCH], F32) for i in range(2)]
        hq = [p4sb(f"hq{i}", [128, 32, FCH], BF16) for i in range(2)]
        sigT = [p4sb(f"sigT{i}", [128, FCH], F32) for i in range(2)]
        sgtm = [p4sb(f"sgtm{i}", [128, 128], F32) for i in range(2)]
        Mab = p4sb("Mab", [128, FCH], F32)
        s2b = p4sb("s2b", [128, FCH], F32)
        s2tmb = p4sb("s2tmb", [128, 128], F32)
        ma2 = p4sb("ma2", [128, 1], F32)
        f2b = [p4sb(f"f2{i}", [128, 1], F32) for i in range(2)]
        ayb = p4sb("ayb", [128, C], F32)
        xrb = p4sb("xrb", [128, C], F32)
        yb = [p4sb(f"yb{i}", [128, C], F32) for i in range(2)]
        sqp = [p4sb(f"sqp{i}", [128, C], F32) for i in range(2)]
        sumsB = p4sb("sumsB", [128, NCH], F32)
        sumsqB = p4sb("sumsqB", [128, NCH], F32)
        mxB = p4sb("mxB", [128, NCH], F32)
        mnB = p4sb("mnB", [128, NCH], F32)
        mvB = p4sb("mvB", [128, NCH], F32)
        vepsB = p4sb("vepsB", [128, NCH], F32)
        lnvB = p4sb("lnvB", [128, NCH], F32)
        rsB = p4sb("rsB", [128, NCH], F32)
        macB = p4sb("macB", [128, NCH], F32)
        sclB = p4sb("sclB", [128, NCH], F32)
        biaB = p4sb("biaB", [128, NCH], F32)
        fmacB = p4sb("fmacB", [128, NCH], F32)
        strip = p4sb("strip", [128, 32, 4], F32)
        tms = p4sb("tms", [128, 32, 2], F32)

        pp = ExitStack()
        ps_m1 = [pp.enter_context(nc.psum_tensor(f"ps_m1{i}", [128, 1024], F32)) for i in range(2)]
        ps_m2 = pp.enter_context(nc.psum_tensor("ps_m2", [128, C], F32))
        ps_tr = pp.enter_context(nc.psum_tensor("ps_tr3", [128, 128], F32))

        bs = dict(cnt)

        @blk.sync
        def _(sy):
            sy = Rec(sy)
            bwait(sy)
            for c in range(0, NCH, 2):
                if c >= 2:
                    sy.wait_ge(sm["b1"], bs["b1"] + c - 1)
                r0 = 127 + FCH * c
                sy.dma_start(out=xf[c % 2][:], in_=d_xffn[r0:r0 + 128, :]).then_inc(sm["dc"], 16)
                cnt["dc"] += 16
            for c in range(NCH):
                if c < 2:
                    sy.wait_ge(sm["b1"], bs["b1"] + NCH)
                else:
                    sy.wait_ge(sm["a5"], bs["a5"] + c - 1)
                r0 = 127 + FCH * c
                sy.dma_start(out=xf[c % 2][:], in_=d_xffn[r0:r0 + 128, :]).then_inc(sm["df"], 16)
                cnt["df"] += 16
                if c >= 1:
                    cc = c - 1
                    sy.wait_ge(sm["c4"], bs["c4"] + cc)
                    r0 = 128 + FCH * cc
                    sy.dma_start(out=ayb[:], in_=d_atny[r0 - 128:r0, :]).then_inc(sm["day"], 16)
                    cnt["day"] += 16
                    sy.dma_start(out=xrb[:], in_=d_xffn[r0:r0 + 128, :]).then_inc(sm["dres"], 16)
                    cnt["dres"] += 16
                    if cc >= 1:
                        sy.dma_start(out=d_y[FCH * (cc - 1):FCH * cc, :],
                                     in_=yb[(cc - 1) % 2][:]).then_inc(sm["dy"], 16)
                        cnt["dy"] += 16
            sy.wait_ge(sm["c4"], bs["c4"] + NFC)
            sy.dma_start(out=d_y[OWN - FCH:OWN, :], in_=yb[(NFC - 1) % 2][:]).then_inc(sm["dy"], 16)
            cnt["dy"] += 16
            sy.wait_ge(sm["dy"], cnt["dy"])

        @blk.vector
        def _(v):
            v = Rec(v)
            bwait(v)
            for c in range(NCH):
                if c % 2 == 0:
                    v.wait_ge(sm["dc"], bs["dc"] + 16 * (c // 2 + 1))
                else:
                    v.wait_ge(sm["dg1"], bs["dg1"] + 16 * ((c + 1) // 2))
                v.reduce_sum(sumsB[:, c:c + 1], xf[c % 2][:], AX.X)
                v.reduce_max(mxB[:, c:c + 1], xf[c % 2][:], AX.X)
                v.tensor_reduce(mnB[:, c:c + 1], xf[c % 2][:], AX.X, ALU.min)
                v.wait_ge(sm["p1"], bs["p1"] + c + 1)
                v.reduce_sum(sumsqB[:, c:c + 1], sqp[c % 2][:], AX.X)
                incl(v, "b1", drain=True)
            v.tensor_scalar(mvB[:], sumsB[:], 1.0 / C, None, ALU.mult)
            v.drain()
            v.tensor_tensor(vepsB[:], mvB[:], mvB[:], ALU.mult)
            v.drain()
            v.scalar_tensor_tensor(vepsB[:], sumsqB[:], 1.0 / C, vepsB[:],
                                   ALU.mult, ALU.subtract)
            v.drain()
            v.tensor_scalar(vepsB[:], vepsB[:], EPS, None, ALU.add)
            incl(v, "a1", drain=True)
            v.wait_ge(sm["p1"], bs["p1"] + NCH + 1)
            v.tensor_tensor(mxB[:], mxB[:], mvB[:], ALU.subtract)
            v.tensor_tensor(mnB[:], mvB[:], mnB[:], ALU.subtract)
            v.drain()
            v.tensor_tensor(macB[:], mxB[:], mnB[:], ALU.max)
            v.drain()
            v.tensor_tensor(macB[:], macB[:], rsB[:], ALU.mult)
            v.drain()
            v.tensor_scalar(macB[:], macB[:], CLIP, None, ALU.max)
            v.drain()
            v.reciprocal(sclB[:], macB[:])
            v.drain()
            v.tensor_scalar(sclB[:], sclB[:], 127.0, None, ALU.mult)
            v.drain()
            v.tensor_tensor(sclB[:], sclB[:], rsB[:], ALU.mult)
            v.drain()
            v.scalar_tensor_tensor(biaB[:], mvB[:], -1.0, sclB[:], ALU.mult, ALU.mult)
            v.drain()
            v.tensor_scalar(biaB[:], biaB[:], MAGIC, None, ALU.add)
            v.tensor_tensor(fmacB[:], macB[:], g["mw1"][:].broadcast_to([128, NCH]), ALU.mult)
            v.drain()
            v.tensor_scalar(fmacB[:], fmacB[:], 1.0 / 127.0, None, ALU.mult)
            v.drain()
            def yb_evac(cc):
                v.wait_ge(sm["c3"], bs["c3"] + cc + 1)
                v.wait_ge(sm["day"], bs["day"] + 16 * (cc + 1))
                v.wait_ge(sm["dres"], bs["dres"] + 16 * (cc + 1))
                if cc >= 2:
                    v.wait_ge(sm["dy"], bs["dy"] + 16 * cc)
                for hb in range(2):
                    sl = slice(hb * 512, (hb + 1) * 512)
                    v.scalar_tensor_tensor(yb[cc % 2][:, sl], ps_m2[:, sl], f2b[cc % 2][:],
                                           ayb[:, sl], ALU.mult, ALU.add)
                    v.tensor_add(yb[cc % 2][:, sl], yb[cc % 2][:, sl], xrb[:, sl])
                incl(v, "c4", drain=True)

            for c in range(NCH):
                if c >= 2:
                    yb_evac(c - 2)
                v.tensor_scalar(sgtm[c % 2][:], g["ones_f"][:], fmacB[:, c:c + 1], None, ALU.mult)
                incl(v, "a3", drain=True)
                v.wait_ge(sm["a4"], bs["a4"] + c + 1)
                v.tensor_scalar(nx[c % 2][:], xf[c % 2][:], MAGIC, None, ALU.subtract)
                incl(v, "a5", drain=True)
                v.wait_ge(sm["a6"], bs["a6"] + 2 * c + 1)
                for b_ in range(8):
                    v.tensor_copy(nxT[c % 2][:, b_, :], ps_m1[0][:, b_ * 128:(b_ + 1) * 128])
                v.wait_ge(sm["a6"], bs["a6"] + 2 * c + 2)
                v.tensor_copy(sigT[c % 2][:], ps_tr[:])
                incl(v, "a7", drain=True)
                if c >= 1:
                    cc = c - 1
                    hc = hwin[cc % 2]
                    cvc = cvo[cc % 2]
                    v.wait_ge(sm["a1"], bs["a1"] + cc + 2)

                def conv_batch(pts):
                    for pt in pts:
                        w1_ = g["cwb"][:, 32 + pt:32 + pt + 1]
                        w2_ = g["cwb"][:, 64 + pt:64 + pt + 1]
                        v.scalar_tensor_tensor(cvc[:, pt, 0:FCH - 2], hc[:, pt, 1:FCH - 1], w1_,
                                               cvc[:, pt, 0:FCH - 2], ALU.mult, ALU.add)
                        v.scalar_tensor_tensor(cvc[:, pt, 0:FCH - 2], hc[:, pt, 2:FCH], w2_,
                                               cvc[:, pt, 0:FCH - 2], ALU.mult, ALU.add)

                for g4 in range(4):
                    if c >= 1:
                        conv_batch(range(8 * g4, 8 * g4 + 8))
                    v.wait_ge(sm["a8"], bs["a8"] + 4 * c + g4 + 1)
                    srep = sigT[c % 2][:].unsqueeze(1).broadcast_to([128, 4, FCH])
                    for hb in range(2):
                        v.tensor_tensor(hwin[c % 2][:, 8 * g4 + 4 * hb:8 * g4 + 4 * hb + 4, :],
                                        ps_m1[g4 % 2][:, hb * 512:(hb + 1) * 512].rearrange("p (a q) -> p a q", a=4),
                                        srep, ALU.mult)
                    incl(v, "b2", drain=True)
                if c >= 1:
                    v.tensor_copy(strip[:, :, 0:2], hc[:, :, FCH - 2:FCH])
                    v.tensor_copy(strip[:, :, 2:4], hwin[c % 2][:, :, 0:2])
                    v.drain()
                    w0s = g["cwb"][:, 0:32].unsqueeze(2).broadcast_to([128, 32, 2])
                    w1s = g["cwb"][:, 32:64].unsqueeze(2).broadcast_to([128, 32, 2])
                    w2s = g["cwb"][:, 64:96].unsqueeze(2).broadcast_to([128, 32, 2])
                    bbs = g["cwb"][:, 96:128].unsqueeze(2).broadcast_to([128, 32, 2])
                    cvs = cvc[:, :, FCH - 2:FCH]
                    v.tensor_tensor(cvs, strip[:, :, 0:2], w0s, ALU.mult)
                    v.tensor_tensor(tms[:], strip[:, :, 1:3], w1s, ALU.mult)
                    v.tensor_add(cvs, cvs, tms[:])
                    v.tensor_tensor(tms[:], strip[:, :, 2:4], w2s, ALU.mult)
                    v.tensor_add(cvs, cvs, tms[:])
                    v.tensor_add(cvs, cvs, bbs)
                    incl(v, "b3", drain=True)
                    v.wait_ge(sm["b4"], bs["b4"] + cc + 1)
                    v.tensor_reduce(Mab[:].unsqueeze(2), cvc[:].transpose([0, 2, 1]),
                                    AX.X, ALU.max, apply_absolute_value=True)
                    incl(v, "b5", drain=True)
                    v.wait_ge(sm["c1"], bs["c1"] + cc + 1)
                    v.reduce_max(ma2[:], ps_tr[:], AX.X)
                    v.drain()
                    v.tensor_scalar(ma2[:], ma2[:], CLIP, None, ALU.max)
                    v.drain()
                    v.tensor_scalar(f2b[cc % 2][:], ma2[:], g["mw2"][:], 1.0 / 127.0, ALU.mult, ALU.mult)
                    v.reciprocal(ma2[:], ma2[:])
                    v.drain()
                    v.tensor_scalar(s2tmb[:], g["ones_f"][:], ma2[:], 127.0, ALU.mult, ALU.mult)
                    incl(v, "b6", drain=True)
                    v.wait_ge(sm["c5"], bs["c5"] + cc + 1)
                    v.tensor_copy(s2b[:], ps_tr[:])
                    s2rep = s2b[:].unsqueeze(1).broadcast_to([128, 32, FCH])
                    if cc >= 2:
                        v.wait_ge(sm["c3"], bs["c3"] + cc - 1)
                    v.tensor_tensor(cvc[:], cvc[:], s2rep, ALU.mult)
                    v.tensor_scalar(hq[cc % 2][:], cvc[:], MAGIC, MAGIC, ALU.add, ALU.subtract)
                    incl(v, "b8", drain=True)
            yb_evac(NFC - 1)

        @blk.scalar
        def _(a):
            a = Rec(a)
            bwait(a)
            for c in range(NCH):
                if c % 2 == 0:
                    a.wait_ge(sm["dc"], bs["dc"] + 16 * (c // 2 + 1))
                else:
                    a.wait_ge(sm["dg1"], bs["dg1"] + 16 * ((c + 1) // 2))
                if c >= 2:
                    a.wait_ge(sm["b1"], bs["b1"] + c - 1)
                a.activation(sqp[c % 2][:], xf[c % 2][:], AF.Square)
                incl(a, "p1", drain=True)
            a.wait_ge(sm["a1"], bs["a1"] + 1)
            a.activation(lnvB[:], vepsB[:], AF.Ln)
            a.drain()
            a.activation(rsB[:], lnvB[:], AF.Exp, scale=-0.5)
            incl(a, "p1", drain=True)
            for c in range(NCH):
                if c >= 1:
                    cc = c - 1
                    a.wait_ge(sm["b2"], bs["b2"] + 4 * (cc + 1))
                    if cc >= 2:
                        a.wait_ge(sm["b8"], bs["b8"] + cc - 1)
                    cvc = cvo[cc % 2]
                    hc = hwin[cc % 2]
                    for pt in range(32):
                        a.activation(cvc[:, pt, 0:FCH - 2], hc[:, pt, 0:FCH - 2], AF.Identity,
                                     bias=g["cwb"][:, 96 + pt:96 + pt + 1],
                                     scale=g["cwb"][:, pt:pt + 1])
                    incl(a, "a1", drain=True)
                a.wait_ge(sm["df"], bs["df"] + 16 * min(c + 2, NCH))
                a.wait_ge(sm["a3"], bs["a3"] + c + 1)
                a.activation(xf[c % 2][:], xf[c % 2][:], AF.Identity,
                             bias=biaB[:, c:c + 1], scale=sclB[:, c:c + 1])
                incl(a, "a4", drain=True)
                if c >= 1:
                    cc = c - 1
                    a.wait_ge(sm["b3"], bs["b3"] + cc + 1)
                    a.activation(cvo[cc % 2][:], cvo[cc % 2][:], AF.Silu)
                    incl(a, "b4", drain=True)

        @blk.tensor
        def _(te):
            te = Rec(te)
            bwait(te)
            for c in range(NCH):
                te.wait_ge(sm["a5"], bs["a5"] + c + 1)
                te.wait_ge(sm["a7"], bs["a7"] + c)
                if c >= 2:
                    te.wait_ge(sm["b8"], bs["b8"] + c - 1)
                if c >= 1:
                    te.wait_ge(sm["b2"], bs["b2"] + 4 * c)
                for b_ in range(8):
                    te.matmul(ps_m1[0][:, b_ * 128:(b_ + 1) * 128],
                              nx[c % 2][:, b_ * 128:(b_ + 1) * 128], g["idb"][:],
                              start=True, stop=True)
                incl(te, "a6")
                te.matmul(ps_tr[:], sgtm[c % 2][:], g["idf"][:], start=True, stop=True)
                incl(te, "a6")
                te.wait_ge(sm["a7"], bs["a7"] + c + 1)
                for g4 in range(4):
                    if 4 * c + g4 >= 2:
                        te.wait_ge(sm["b2"], bs["b2"] + 4 * c + g4 - 1)
                    for htl in range(8):
                        ht = 8 * g4 + htl
                        for dk in range(8):
                            te.matmul(ps_m1[g4 % 2][:, htl * FCH:(htl + 1) * FCH],
                                      g["w1qT"][:, dk, ht * 128:(ht + 1) * 128],
                                      nxT[c % 2][:, dk, :],
                                      start=(dk == 0), stop=(dk == 7))
                    incl(te, "a8")
                if c >= 1:
                    cc = c - 1
                    te.wait_ge(sm["b5"], bs["b5"] + cc + 1)
                    te.matmul(ps_tr[:], Mab[:], g["idf"][:], start=True, stop=True)
                    incl(te, "c1")
                    te.wait_ge(sm["b6"], bs["b6"] + cc + 1)
                    te.matmul(ps_tr[:], s2tmb[:], g["idf"][:], start=True, stop=True)
                    incl(te, "c5")
                    te.wait_ge(sm["b8"], bs["b8"] + cc + 1)
                    if cc >= 1:
                        te.wait_ge(sm["c4"], bs["c4"] + cc)
                    for kt in range(32):
                        for n in range(2):
                            te.matmul(ps_m2[:, n * 512:(n + 1) * 512],
                                      hq[cc % 2][:, kt, :],
                                      g["w2qT"][:, kt, n * 512:(n + 1) * 512],
                                      start=(kt == 0), stop=(kt == 31))
                    incl(te, "c3")

        @blk.gpsimd
        def _(gp):
            gp = Rec(gp)
            bwait(gp)
            for c in range(1, NCH, 2):
                gp.wait_ge(sm["b1"], bs["b1"] + c - 1)
                r0 = 127 + FCH * c
                gp.dma_start(out=xf[c % 2][:], in_=d_xffn[r0:r0 + 128, :]).then_inc(sm["dg1"], 16)
                cnt["dg1"] += 16
            gp.wait_ge(sm["dg1"], cnt["dg1"])

        if "ffn" in dbg:
            @blk.gpsimd
            def _(gp):
                gp = Rec(gp)
                gp.wait_ge(sm["dy"], cnt["dy"])
                gp.dma_start(out=dbg_t["w1q"].ap(), in_=g["w1qT"].ap()).then_inc(sm["dbg"], 16)
                gp.dma_start(out=dbg_t["w2q"].ap(), in_=g["w2qT"].ap()).then_inc(sm["dbg"], 16)
                gp.dma_start(out=dbg_t["h1"].ap(), in_=hwin[1][:]).then_inc(sm["dbg"], 16)
                gp.dma_start(out=dbg_t["hq0"].ap(), in_=hq[1][:]).then_inc(sm["dbg"], 16)
                cnt["dbg"] += 64
                gp.wait_ge(sm["dbg"], cnt["dbg"])

        barrier_close()
        pp.close()
        p4.close()

    ph3()
    st.close()
    return nc


def make_inputs(L0_attn_x, L0_attn_Wq, L0_attn_bq, L0_attn_Wk, L0_attn_bk,
                L0_attn_Wv, L0_attn_bv, L0_attn_Wo, L0_attn_bo,
                L0_ffn_w1, L0_ffn_conv_w, L0_ffn_conv_b, L0_ffn_w2):
    import ml_dtypes
    f32 = np.float32
    x = np.asarray(L0_attn_x, f32)
    Wq, Wk, Wv, Wo = (np.asarray(a, f32) for a in (L0_attn_Wq, L0_attn_Wk, L0_attn_Wv, L0_attn_Wo))
    bq, bk, bv, bo = (np.asarray(a, f32) for a in (L0_attn_bq, L0_attn_bk, L0_attn_bv, L0_attn_bo))
    w1 = np.asarray(L0_ffn_w1, f32)
    cw = np.asarray(L0_ffn_conv_w, f32).reshape(DH, 3)
    cb = np.asarray(L0_ffn_conv_b, f32)
    w2 = np.asarray(L0_ffn_w2, f32)

    theta = (10000.0 ** (-np.arange(0, D, 2).astype(f32) / D)).astype(f32)
    freqs = np.arange(T, dtype=f32)[:, None] * theta[None, :]
    cosv, sinv = np.cos(freqs).astype(f32), np.sin(freqs).astype(f32)
    ropeC = np.zeros((T, D), f32)
    ropeS = np.zeros((T, D), f32)
    ropeC[:, 0::2] = cosv; ropeC[:, 1::2] = cosv
    ropeS[:, 0::2] = -sinv; ropeS[:, 1::2] = sinv
    mask = (np.arange(128)[None, :] >= np.arange(128)[:, None]).astype(ml_dtypes.bfloat16)
    idb = np.eye(128, dtype=ml_dtypes.bfloat16)
    idf = np.eye(128, dtype=f32)
    cwb = np.zeros((128, 128), f32)
    for pt in range(32):
        cwb[:, pt] = cw[pt * 128:(pt + 1) * 128, 0]
        cwb[:, 32 + pt] = cw[pt * 128:(pt + 1) * 128, 1]
        cwb[:, 64 + pt] = cw[pt * 128:(pt + 1) * 128, 2]
        cwb[:, 96 + pt] = cb[pt * 128:(pt + 1) * 128]

    w1T = np.ascontiguousarray(w1.T)
    w2T = np.ascontiguousarray(w2.T)
    WoT = np.ascontiguousarray(Wo.T)
    in_maps = []
    for c in range(8):
        b, g_ = c // 2, c % 2
        s0 = g_ * OWN
        xffn = np.zeros((2432, C), f32)
        lo, hi = s0 - 128, s0 + 2304
        slo, shi = max(lo, 0), min(hi, T)
        xffn[slo - lo:shi - lo] = x[b, slo:shi]
        rows = slice(512 * g_, 512 * (g_ + 1))
        wqkvT = np.concatenate([Wq[rows].T, Wk[rows].T, Wv[rows].T], axis=1)
        bqkv = np.concatenate([bq[rows], bk[rows], bv[rows]])[None, :]
        woT = np.zeros((2048, C), f32)
        for r in range(2):
            for blk_ in range(4):
                kt = (r * 2 + g_) * 4 + blk_
                woT[kt * 128:(kt + 1) * 128] = WoT[512 * r + 128 * blk_:512 * r + 128 * (blk_ + 1)]
        in_maps.append({
            "x": np.ascontiguousarray(x[b]),
            "xffn": xffn,
            "wqkvT": np.ascontiguousarray(wqkvT),
            "bqkv": np.ascontiguousarray(bqkv),
            "woT": woT,
            "bo_bc": np.broadcast_to(bo[None, :], (128, C)).copy(),
            "w1T": w1T,
            "w2T": w2T,
            "convwb": cwb,
            "ropeC": np.ascontiguousarray(ropeC.reshape(NT, 128, D).transpose(1, 0, 2).reshape(128, NT * D)),
            "ropeS": np.ascontiguousarray(ropeS.reshape(NT, 128, D).transpose(1, 0, 2).reshape(128, NT * D)),
            "masktri": mask, "ident_bf": idb, "ident_f32": idf,
        })
    return in_maps


def kernel(**inputs):
    in_maps = make_inputs(**inputs)
    nc = build_program(phases=4)
    res = run_bass_kernel_spmd(nc, in_maps, core_ids=list(range(8)))
    y = np.zeros((B, T, C), np.float32)
    for c in range(8):
        b, g_ = c // 2, c % 2
        y[b, g_ * OWN:(g_ + 1) * OWN] = res.results[c]["y"]
    return y
